# version 29
# baseline (speedup 1.0000x reference)
import numpy as np
import ml_dtypes
import concourse.bass as bass
import concourse.mybir as mybir
from concourse.bass_utils import run_bass_kernel_spmd

F32 = mybir.dt.float32
BF16 = mybir.dt.bfloat16
AF = mybir.ActivationFunctionType
ALU = mybir.AluOpType
AXX = mybir.AxisListType.X
bf = ml_dtypes.bfloat16

D = 2048
T = 2048
NSEQ = 2
H = 16
HD = 128
C = 128
NCH = 16
HG = 2
NHG = H // HG
TH = 1024
KT = 16
EPS_LN = 1e-5
EPS_RMS = 1e-6

ENGINES = ("tensor", "vector", "scalar", "gpsimd", "sync")


class Prog:

    def __init__(self):
        self.ops = {e: [] for e in ENGINES}
        self.n = {}
        self.hw = {}
        self.last = {}

    def sem(self, name):
        self.n[name] = 0

    def add(self, eng, fn, deps=(), inc=None, amt=1):
        ws = []
        flat = []
        for d in deps:
            if isinstance(d, list):
                flat.extend(d)
            else:
                flat.append(d)
        for d in flat:
            if d is None:
                continue
            s, c = d
            if c <= 0:
                continue
            k = (eng, s)
            if c > self.hw.get(k, 0):
                self.hw[k] = c
                ws.append((s, c))
        tok = None
        if inc is not None:
            self.n[inc] += amt
            tok = (inc, self.n[inc])
            self.last[inc] = tok
        self.ops[eng].append((fn, ws, inc, amt))
        return tok

    def barrier(self, engines=("tensor", "vector", "scalar")):
        toks = [self.last.get(s) for s in ("T", "V", "S")]
        for e in engines:
            self.add(e, None, deps=toks)


def build_program(debug=False):
    nc = bass.Bass("TRN2", target_bir_lowering=False)

    dx = nc.dram_tensor("x", [NSEQ, T, D], F32, kind="ExternalInput").ap()
    dout = nc.dram_tensor("out", [NSEQ, T, D], F32, kind="ExternalOutput").ap()
    dWq = nc.dram_tensor("WqS", [128, H, KT, 128], F32, kind="ExternalInput").ap()
    dWk = nc.dram_tensor("WkS", [128, H, KT, 128], F32, kind="ExternalInput").ap()
    dWv = nc.dram_tensor("WvS", [128, NHG, KT, 256], F32, kind="ExternalInput").ap()
    dWo = nc.dram_tensor("WoS", [128, KT, D], F32, kind="ExternalInput").ap()
    dWg1 = nc.dram_tensor("Wg1", [D, HD], F32, kind="ExternalInput").ap()
    dWg2 = nc.dram_tensor("Wg2", [HD, D], F32, kind="ExternalInput").ap()
    dBWT = nc.dram_tensor("baseWT", [D, D], F32, kind="ExternalInput").ap()
    dARC = nc.dram_tensor("ARcat", [D, 136], F32, kind="ExternalInput").ap()
    dBC = nc.dram_tensor("Bcat", [128, D], F32, kind="ExternalInput").ap()
    dBB = nc.dram_tensor("baseb", [1, D], F32, kind="ExternalInput").ap()
    dRB = nc.dram_tensor("rbb", [128, 8], F32, kind="ExternalInput").ap()
    dONE = nc.dram_tensor("onesr", [1, 128], BF16, kind="ExternalInput").ap()
    dNW = nc.dram_tensor("nwT", [128, 16], F32, kind="ExternalInput").ap()
    dID = nc.dram_tensor("ident", [128, 128], BF16, kind="ExternalInput").ap()
    dCOS = nc.dram_tensor("cosF", [128, T], BF16, kind="ExternalInput").ap()
    dSIN = nc.dram_tensor("sinF", [128, T], BF16, kind="ExternalInput").ap()
    dDT = nc.dram_tensor("DTm", [128, H * C], BF16, kind="ExternalInput").ap()
    dDQ = nc.dram_tensor("DQm", [128, H * C], BF16, kind="ExternalInput").ap()
    dDK = nc.dram_tensor("dkc", [128, H], F32, kind="ExternalInput").ap()
    s_x2 = nc.dram_tensor("scr_x2", [T, D], F32, kind="Internal").ap()
    s_o = nc.dram_tensor("scr_o", [NCH, 128, KT * 128], BF16, kind="Internal").ap()
    s_l2 = nc.dram_tensor("scr_l2", [NCH, 128, KT * 128], BF16, kind="Internal").ap()

    lam = -(2.0 ** (-8.0 * (np.arange(H, dtype=np.float64) + 1.0) / H))
    lamC = [float(np.exp(l * C)) for l in lam]

    import contextlib
    es = contextlib.ExitStack()
    p = Prog()
    for s in ("T", "V", "S", "DX", "DW", "DS", "DP"):
        p.sem(s)

    with es:
        sbuf = lambda n, sh, dt: es.enter_context(nc.sbuf_tensor(n, sh, dt))
        lnT = sbuf("lnT", [128, KT * TH], BF16)
        WB = sbuf("WB", [128, KT * D], BF16)
        xt = [sbuf(f"xt{i}", [128, D], F32) for i in range(2)]
        lnbf = [sbuf(f"lnbf{i}", [128, D], BF16) for i in range(3)]
        qT = sbuf("qT", [128, HG * TH], BF16)
        kT = sbuf("kT", [128, HG * TH], BF16)
        dkc = sbuf("dkcs", [128, H], F32)
        nwc = sbuf("nwcs", [128, 16], F32)
        ident = sbuf("idents", [128, 128], BF16)
        onesr = sbuf("onesrs", [1, 128], BF16)
        rbs = sbuf("rbs", [128, 8], F32)
        wpf = sbuf("wpf", [128, 2048], BF16)
        cols = sbuf("cols", [128, 48], F32)
        lg8 = sbuf("lg8", [128, 8], F32)
        g8a = sbuf("g8a", [128, 8], F32)
        g8b = sbuf("g8b", [128, 8], F32)
        gat = sbuf("gat", [128, 8], F32)
        univ = sbuf("univ", [128, 14096], F32)
        ub = univ.bitcast(BF16)
        Sst = univ[:, 0:2048]
        xr = [univ[:, 2048 + 256 * i: 2048 + 256 * (i + 1)] for i in range(2)]
        wbig = ub[:, 5120:9216]
        st0 = [ub[:, 9216 + 512 * i: 9216 + 512 * (i + 1)] for i in range(2)]
        rt1 = ub[:, 10240:10752]
        rt2 = ub[:, 10752:11264]
        vb = [ub[:, 11264 + 256 * i: 11264 + 256 * (i + 1)] for i in range(2)]
        scm = [ub[:, 11776 + 256 * i: 11776 + 256 * (i + 1)] for i in range(2)]
        qdq = [ub[:, 12288 + 256 * i: 12288 + 256 * (i + 1)] for i in range(2)]
        kdk = [ub[:, 12800 + 256 * i: 12800 + 256 * (i + 1)] for i in range(2)]
        Sb = [ub[:, 13312 + 256 * i: 13312 + 256 * (i + 1)] for i in range(2)]
        opn = [ub[:, 13824 + 256 * i: 13824 + 256 * (i + 1)] for i in range(2)]
        osp = [ub[:, 14336 + 256 * i: 14336 + 256 * (i + 1)] for i in range(2)]
        dum2 = ub[:, 14848:15104]
        cosH = ub[:, 15104:16128]
        sinH = ub[:, 16128:17152]
        DTm = ub[:, 17152:19200]
        DQm = ub[:, 19200:21248]
        midT = ub[:, 21248:23296]
        wg1 = ub[:, 23296:25344]
        wg2 = ub[:, 25344:27392]
        dSb = [ub[:, 27424 + 256 * i: 27424 + 256 * (i + 1)] for i in range(2)]
        Sbsd = ub[:, 27936:28192]
        ao = [univ[:, 2048 * i: 2048 * (i + 1)] for i in range(2)]
        oTb = [ub[:, 8192 + 2048 * i: 8192 + 2048 * (i + 1)] for i in range(2)]
        arc = ub[:, 12544:14720]
        p0snk = ub[:, 12544:14592]
        bcat = ub[:, 14720:16768]
        bbr = ub[0:1, 16768:18816]
        snk = ub[:, 19200:21248]
        mgt = ub[:, 21248:23296]
        mid16 = mgt
        lgall = univ[:, 13712:13840]
        mde2 = [ub[:, 27936 + 128 * i: 27936 + 128 * (i + 1)] for i in range(2)]
        sgm = xt[0].bitcast(BF16)

        bank = [es.enter_context(nc.psum_tensor(f"bk{i}", [128, 512], F32))
                for i in range(8)]
        BK_V, BK_S0, BK_S1, BK_O0, BK_O1, BK_DS, BK_KT, BK_OT = range(8)

        sems = {}
        for s in ("T", "V", "S", "DX", "DW", "DS", "DP"):
            sems[s] = es.enter_context(nc.semaphore(s))

        col_i = [0]

        def col():
            col_i[0] = (col_i[0] + 1) % 48
            return cols[:, col_i[0]:col_i[0] + 1]

        ldtok = []
        for dst, srcd in ((ident, dID), (dkc, dDK), (nwc, dNW), (onesr, dONE)):
            ldtok.append(p.add("sync", lambda y, d=dst, s=srcd: y.dma_start(
                out=d[:, :], in_=s), inc="DX", amt=16))

        def load_tables(dep):
            t1 = p.add("sync", lambda y: y.dma_start(out=DTm, in_=dDT),
                       deps=dep, inc="DX", amt=16)
            t2 = p.add("sync", lambda y: y.dma_start(out=DQm, in_=dDQ),
                       deps=dep, inc="DX", amt=16)
            return [t1, t2]
        consts_tok = ldtok
        wtok = [None, None]
        arc_tok = [None]; bc_tok = [None]; bb_tok = [None]; rb_tok = [None]

        def load_gate_weights(dep):
            wtok[0] = p.add("gpsimd", lambda g: g.dma_start(
                out=wg1.rearrange("p (k n) -> p k n", k=KT),
                in_=dWg1.rearrange("(k p) n -> p k n", p=128)),
                deps=[dep], inc="DW", amt=16)
            wtok[1] = p.add("gpsimd", lambda g: g.dma_start(
                out=wg2, in_=dWg2), deps=[dep], inc="DW", amt=16)

        rb_tok[0] = p.add("gpsimd", lambda g: g.dma_start(
            out=rbs[:, :], in_=dRB), inc="DW", amt=16)
        wpf_state = {"rd": None}
        wpf_state["tok"] = p.add("gpsimd", lambda g: g.dma_start(
            out=wpf[:, :].rearrange("p (k n) -> p k n", k=KT),
            in_=dWq[:, 0, :, :]),
            inc="DP", amt=16)

        def load_ffn_weights(dep):
            arc_tok[0] = p.add("gpsimd", lambda g: g.dma_start(
                out=arc.rearrange("p (k n) -> p k n", k=KT),
                in_=dARC.rearrange("(k p) n -> p k n", p=128)),
                deps=[dep], inc="DW", amt=16)
            bc_tok[0] = p.add("gpsimd", lambda g: g.dma_start(
                out=bcat, in_=dBC), deps=[dep], inc="DW", amt=16)
            bb_tok[0] = p.add("gpsimd", lambda g: g.dma_start(
                out=bbr, in_=dBB), deps=[dep], inc="DW", amt=16)

        bk_w = [None] * 8
        bk_r = [None] * 8

        def mm_round(bki, parts, deps, tag=""):
            tok = None
            first = True
            for i, (o, l, r, st, sp) in enumerate(parts):
                dd = list(deps) + [bk_r[bki]] if first else []
                last = (i == len(parts) - 1)
                tok = p.add("tensor",
                            lambda t, o=o, l=l, r=r, st=st, sp=sp: t.matmul(
                                o, l, r, start=st, stop=sp),
                            deps=dd, inc="T" if last else None)
                first = False
            bk_w[bki] = tok
            bk_r[bki] = None
            return tok

        def evac(eng, bki, fn, deps=()):
            tok = p.add(eng, fn, deps=list(deps) + [bk_w[bki]],
                        inc={"vector": "V", "scalar": "S"}[eng])
            bk_r[bki] = tok
            return tok

        xt_free = [None, None]
        ao_free = [None, None]
        oTb_free = [None, None]
        wbig_free = [None, None]

        def frees():
            return [p.last.get("T"), p.last.get("V"), p.last.get("S")]

        out_dma_count = [0]

        def phase_p0(s, Hh, first_p0):
            ev_last = None
            fz = frees()
            lt = p.last.get("T")
            p0_rd = [lt, lt]
            mtoks = []
            for tl in range(8):
                tg = Hh * 8 + tl
                pr = tl % 2
                tok_x = p.add("sync", lambda y, s=s, tg=tg, pr=pr: y.dma_start(
                    out=xt[pr][:, :], in_=dx[s, tg * 128:(tg + 1) * 128, :]),
                    deps=[xt_free[pr]] + fz, inc="DX", amt=16)
                sx = col()
                tv1 = p.add("vector", lambda v, pr=pr, sx=sx: v.reduce_sum(
                    sx, xt[pr][:, :], axis=AXX), deps=[tok_x], inc="V")
                sq = col()
                ts1 = p.add("scalar", lambda ss, pr=pr, sq=sq: ss.activation(
                    p0snk[:, :], xt[pr][:, :], AF.Square, accum_out=sq),
                    deps=[tv1], inc="S")
                m = col()
                tv2 = p.add("vector", lambda v, sx=sx, m=m: v.tensor_scalar(
                    m, sx, 1.0 / D, 0.0, ALU.mult, ALU.add), deps=[tv1], inc="V")
                msq = col()
                tv3 = p.add("vector", lambda v, m=m, msq=msq: v.tensor_scalar(
                    msq, m, m, None, ALU.mult), deps=[tv2], inc="V")
                var = col()
                tv4 = p.add("vector", lambda v, sq=sq, msq=msq, var=var:
                            v.scalar_tensor_tensor(var, sq, 1.0 / D, msq,
                                                   ALU.mult, ALU.subtract),
                            deps=[ts1, tv3], inc="V")
                sd = col()
                ts2 = p.add("scalar", lambda ss, var=var, sd=sd: ss.activation(
                    sd, var, AF.Sqrt, bias=np.float32(EPS_LN)), deps=[tv4], inc="S")
                rs = col()
                tv5 = p.add("vector", lambda v, sd=sd, rs=rs: v.reciprocal(
                    rs, sd), deps=[ts2], inc="V")
                tap = p.add("vector", lambda v, pr=pr, m=m, rs=rs:
                            v.tensor_scalar(lnbf[pr][:, :], xt[pr][:, :], m, rs,
                                            ALU.subtract, ALU.mult),
                            deps=[tv5, ts1, p0_rd[pr]], inc="V")
                xt_free[pr] = tap
                for r in range(4):
                    bki = BK_S0 if r % 2 == 0 else BK_S1
                    parts = []
                    for j in range(4):
                        kk = 4 * r + j
                        parts.append((bank[bki][:, j * 128:(j + 1) * 128],
                                      lnbf[pr][:, kk * 128:(kk + 1) * 128],
                                      ident[:, :], True, True))
                    dd = [tap] + (consts_tok if first_p0 else [])
                    mm_round(bki, parts, dd)
                    dst = lnT[:, :].rearrange(
                        "p (k n) -> p k n", k=KT)[:, 4 * r:4 * r + 4,
                                                  tl * 128:tl * 128 + 128]
                    ev_last = evac("scalar", bki, lambda ss, bki=bki, dst=dst:
                                   ss.copy(dst, bank[bki][:, :].rearrange(
                                       "p (k n) -> p k n", k=4)),
                                   deps=[lt])
                p0_rd[pr] = p.last["T"]
                if tl in (3, 7):
                    nt = tl // 4
                    bki = BK_O0 if nt == 0 else BK_O1
                    parts = []
                    for k in range(KT):
                        parts.append((bank[bki][:, :],
                                      wg1[:, k * 128:(k + 1) * 128],
                                      lnT[:, k * TH + nt * 512:
                                          k * TH + nt * 512 + 512],
                                      k == 0, k == KT - 1))
                    mm_round(bki, parts, [ev_last, wtok[0]])
                    mtoks.append(evac(
                        "vector", bki, lambda v, bki=bki, Hh=Hh, nt=nt:
                        v.tensor_copy(midT[:, Hh * TH + nt * 512:
                                           Hh * TH + nt * 512 + 512],
                                      bank[bki][:, :]), deps=fz))
            return [ev_last] + mtoks

        def phase_attn(s, Hh, scan_state):
            fz = frees()
            ct = p.add("sync", lambda y, Hh=Hh: y.dma_start(
                out=cosH[:, :], in_=dCOS[:, Hh * TH:(Hh + 1) * TH]),
                deps=[scan_state.get("rope_last")] + fz, inc="DX", amt=16)
            ct2 = p.add("sync", lambda y, Hh=Hh: y.dma_start(
                out=sinH[:, :], in_=dSIN[:, Hh * TH:(Hh + 1) * TH]),
                deps=[scan_state.get("rope_last")] + fz, inc="DX", amt=16)
            costok = [ct, ct2]
            st0_free = scan_state.setdefault("st0_free", [None, None])
            osp_dma = scan_state.setdefault("osp_dma", [None, None])
            opn_free = scan_state.setdefault("opn_free", [None, None])
            pend_otrs = scan_state.setdefault("pend_otrs", [])
            snaps = scan_state.setdefault("snaps", [])
            for hg in range(NHG):
                rope_tok = {"q": [[None, None], [None, None]],
                            "k": [[None, None], [None, None]]}
                pr_s = 0
                rope_prev = scan_state.get("rope_prev")
                scan_prev = scan_state.get("scan_last")
                for wname, wd, dstT in (("q", dWq, qT), ("k", dWk, kT)):
                    for h in range(HG):
                        gh = hg * HG + h
                        if wname == "q" and h == 0:
                            wv = wpf[:, :]
                            wt = wpf_state["tok"]
                            reg = None
                        else:
                            reg = (2 * (0 if wname == "q" else 1) + h) % 2
                            wv = wbig[:, reg * 2048:(reg + 1) * 2048]
                            wt = p.add("gpsimd", lambda g, wd=wd, gh=gh, wv=wv:
                                       g.dma_start(
                                           out=wv.rearrange("p (k n) -> p k n", k=KT),
                                           in_=wd[:, gh, :, :]),
                                       deps=[wbig_free[reg]], inc="DW", amt=16)
                        wt_cum = ("DW", p.n["DW"])
                        for nt in range(2):
                            bki = BK_O0 if pr_s % 2 == 0 else BK_O1
                            pr_s += 1
                            parts = []
                            for k in range(KT):
                                parts.append((bank[bki][:, :],
                                              wv[:, k * 128:(k + 1) * 128],
                                              lnT[:, k * TH + nt * 512:
                                                  k * TH + nt * 512 + 512],
                                              k == 0, k == KT - 1))
                            mmt = mm_round(bki, parts, [wt, wt_cum, scan_prev]
                                           + scan_state["p0_done"])
                            if reg is None:
                                wpf_state["rd"] = mmt
                            else:
                                wbig_free[reg] = mmt
                            sp = pr_s % 2
                            tsl = evac("scalar", bki, lambda ss, bki=bki, sp=sp:
                                       ss.activation(st0[sp][:, :],
                                                     bank[bki][:, :], AF.Silu),
                                       deps=[st0_free[sp]])
                            dsl = dstT[:, h * TH + nt * 512: h * TH + nt * 512 + 512]
                            cs = cosH[:, nt * 512:nt * 512 + 512]
                            sn = sinH[:, nt * 512:nt * 512 + 512]
                            v1 = p.add("vector", lambda v, sp=sp: v.tensor_scalar(
                                rt2[0:64, :], st0[sp][64:128, :], -1.0, None,
                                ALU.mult), deps=[tsl, rope_prev, scan_prev],
                                inc="V")
                            v2 = p.add("vector", lambda v, sp=sp: v.tensor_copy(
                                rt2[64:128, :], st0[sp][0:64, :]),
                                deps=[tsl], inc="V")
                            v3 = p.add("vector", lambda v, sp=sp, cs=cs:
                                       v.tensor_mul(rt1[:, :], st0[sp][:, :], cs),
                                       deps=[tsl] + costok, inc="V")
                            st0_free[sp] = v3
                            v4 = p.add("vector", lambda v, sn=sn: v.tensor_mul(
                                rt2[:, :], rt2[:, :], sn), deps=[v1, v2],
                                inc="V")
                            v5 = p.add("vector", lambda v, dsl=dsl: v.tensor_add(
                                dsl, rt1[:, :], rt2[:, :]), deps=[v3, v4],
                                inc="V")
                            rope_prev = v5
                            rope_tok[wname][h][nt] = v5
                scan_state["rope_prev"] = rope_prev
                scan_state["rope_last"] = rope_prev
                sgm_tok = [None] * 8

                def gate_tail(c):
                    bki = BK_DS if c % 2 == 0 else BK_KT
                    xp = c % 2
                    xdep = [sgm_tok[c - 2]] if c >= 2 else [p.last.get("V")]
                    th = evac("scalar", bki, lambda ss, bki=bki, xp=xp:
                              ss.activation(xr[xp][:, :], bank[bki][:, 0:256],
                                            AF.Tanh, scale=np.float32(0.5)),
                              deps=xdep)
                    sgm_tok[c] = p.add("vector", lambda v, xp=xp, c=c:
                                       v.tensor_scalar(
                                           sgm[:, c * 256:(c + 1) * 256],
                                           xr[xp][:, :], 0.5, 0.5, ALU.mult,
                                           ALU.add), deps=[th, scan_prev],
                                       inc="V")
                for c in range(8):
                    gc = Hh * 8 + c
                    bki = BK_DS if c % 2 == 0 else BK_KT
                    parts = [(bank[bki][0:128, 0:256],
                              midT[:, gc * 128:(gc + 1) * 128],
                              wg2[:, hg * 256:(hg + 1) * 256], True, True)]
                    mm_round(bki, parts, [wtok[1], scan_prev]
                             + scan_state["p0_done"])
                    if c >= 1:
                        gate_tail(c - 1)
                gate_tail(7)
                wvt = p.add("gpsimd", lambda g, hg=hg: g.dma_start(
                    out=wbig[:, :].rearrange("p (k n) -> p k n", k=KT),
                    in_=dWv[:, hg, :, :]),
                    deps=[wbig_free[0], wbig_free[1]], inc="DW", amt=16)
                nxt_gh = ((hg + 1) % NHG) * HG
                wpf_state["tok"] = p.add(
                    "gpsimd", lambda g, nxt_gh=nxt_gh: g.dma_start(
                        out=wpf[:, :].rearrange("p (k n) -> p k n", k=KT),
                        in_=dWq[:, nxt_gh, :, :]),
                    deps=[wpf_state["rd"]], inc="DP", amt=16)
                seed_tok = None
                if Hh == 1:
                    seed_tok = p.add("vector", lambda v, hg=hg: v.tensor_copy(
                        Sbsd[:, :], Sst[:, hg * 256:(hg + 1) * 256]),
                        deps=[scan_prev, scan_state.get("sb_tok")], inc="V")
                    for h in range(HG):
                        gh = hg * HG + h
                        seed_tok = p.add("scalar", lambda ss, gh=gh, h=h:
                                         ss.mul(Sb[1][:, h * 128:(h + 1) * 128],
                                                Sst[:, gh * 128:(gh + 1) * 128],
                                                float(lamC[gh])),
                                         deps=[scan_prev, seed_tok,
                                               scan_state.get("sb_tok")],
                                         inc="S")
                    scan_state["sb_tok"] = seed_tok
                last_v_mm = None
                for c in range(8):
                    gc = Hh * 8 + c
                    par = gc % 2
                    nt = c // 4
                    sn = snaps[-2] if len(snaps) >= 2 else []
                    parts = []
                    for k in range(KT):
                        parts.append((bank[BK_V][:, 0:256],
                                      lnT[:, k * TH + c * 128: k * TH + c * 128 + 128],
                                      wbig[:, k * 256:(k + 1) * 256],
                                      k == 0, k == KT - 1))
                    mm_round(BK_V, parts, [wvt] + sn)
                    ve = evac("scalar", BK_V, lambda ss, par=par: ss.copy(
                        vb[par][:, :], bank[BK_V][:, 0:256]), deps=sn)
                    parts = []
                    for h in range(HG):
                        qs = qT[:, h * TH + c * 128: h * TH + c * 128 + 128]
                        ks = kT[:, h * TH + c * 128: h * TH + c * 128 + 128]
                        parts.append((bank[BK_S0 + par][:, h * 128:(h + 1) * 128],
                                      ks, qs, h == 0, h == HG - 1))
                    mm_round(BK_S0 + par, parts,
                             [rope_tok["q"][0][nt], rope_tok["q"][1][nt],
                              rope_tok["k"][0][nt], rope_tok["k"][1][nt]])
                    msk = evac("vector", BK_S0 + par, lambda v, par=par, hg=hg:
                               v.tensor_mul(scm[par][:, :],
                                            bank[BK_S0 + par][:, 0:256],
                                            DTm[:, hg * 256:(hg + 1) * 256]),
                               deps=scan_state["tab_tok"] + sn)
                    kdk_tok = None
                    if gc != NCH - 1:
                        parts = []
                        for h in range(HG):
                            ks = kT[:, h * TH + c * 128: h * TH + c * 128 + 128]
                            parts.append((bank[BK_KT][:, h * 128:(h + 1) * 128],
                                          ks, ident[:, :], True, True))
                        mm_round(BK_KT, parts, [rope_tok["k"][0][nt],
                                                rope_tok["k"][1][nt]])
                        for h in range(HG):
                            gh = hg * HG + h
                            kdk_tok = evac("scalar", BK_KT,
                                           lambda ss, par=par, h=h, gh=gh:
                                           ss.mul(
                                               kdk[par][:, h * 128:(h + 1) * 128],
                                               bank[BK_KT][:, h * 128:(h + 1) * 128],
                                               dkc[:, gh:gh + 1]),
                                           deps=[] if h == 0 else [])
                    while len(pend_otrs) >= 2:
                        pend_otrs.pop(0)()
                    qdq_tok = None
                    if gc != 0:
                        q3 = qT[:, :].rearrange("p (h n) -> p h n", h=HG)[
                            :, :, c * 128:c * 128 + 128]
                        d3 = DQm[:, hg * 256:(hg + 1) * 256].rearrange(
                            "p (h n) -> p h n", h=HG)
                        o3 = qdq[par][:, :].rearrange("p (h n) -> p h n", h=HG)
                        qdq_tok = p.add("vector", lambda v, q3=q3, d3=d3, o3=o3:
                                        v.tensor_mul(o3, q3, d3),
                                        deps=[rope_tok["q"][0][nt],
                                              rope_tok["q"][1][nt]]
                                        + scan_state["tab_tok"], inc="V")
                    parts = []
                    for h in range(HG):
                        parts.append((bank[BK_O0 + par][:, h * 128:(h + 1) * 128],
                                      scm[par][:, h * 128:(h + 1) * 128],
                                      vb[par][:, h * 128:(h + 1) * 128],
                                      h == 0, False))
                    if gc != 0:
                        if Hh == 1 and c == 0:
                            for h in range(HG):
                                parts.append((bank[BK_O0 + par][:, h * 128:(h + 1) * 128],
                                              qdq[par][:, h * 128:(h + 1) * 128],
                                              Sbsd[:, h * 128:(h + 1) * 128],
                                              False, False))
                        else:
                            for h in range(HG):
                                parts.append((bank[BK_O0 + par][:, h * 128:(h + 1) * 128],
                                              qdq[par][:, h * 128:(h + 1) * 128],
                                              dSb[(gc + 1) % 2][:, h * 128:(h + 1) * 128],
                                              False, False))
                            if gc >= 2:
                                for h in range(HG):
                                    parts.append((bank[BK_O0 + par][:, h * 128:(h + 1) * 128],
                                                  qdq[par][:, h * 128:(h + 1) * 128],
                                                  Sb[gc % 2][:, h * 128:(h + 1) * 128],
                                                  False, False))
                    o0, l0, r0, s0f, _ = parts[-1]
                    parts[-1] = (o0, l0, r0, s0f, True)
                    odeps = [msk, ve, qdq_tok,
                             scan_state.get("dsb_tok") if gc != 0 else None,
                             scan_state.get("sb2_tok") if gc >= 2 else None,
                             seed_tok if (Hh == 1 and c == 0) else None]
                    mm_round(BK_O0 + par, parts, odeps)
                    if gc != NCH - 1:
                        parts = []
                        for h in range(HG):
                            parts.append((bank[BK_DS][:, h * 128:(h + 1) * 128],
                                          kdk[par][:, h * 128:(h + 1) * 128],
                                          vb[par][:, h * 128:(h + 1) * 128],
                                          h == 0, h == HG - 1))
                        mm_round(BK_DS, parts, [kdk_tok, ve])
                        dsbt = evac("scalar", BK_DS, lambda ss, par=par:
                                    ss.copy(dSb[par][:, :], bank[BK_DS][:, 0:256]),
                                    deps=[scan_state.get("dsb_rd")])
                        scan_state["dsb_tok"] = dsbt
                        up_tok = None
                        for h in range(HG):
                            gh = hg * HG + h
                            up_tok = p.add(
                                "vector", lambda v, gh=gh, h=h:
                                v.scalar_tensor_tensor(
                                    Sst[:, gh * 128:(gh + 1) * 128],
                                    Sst[:, gh * 128:(gh + 1) * 128],
                                    lamC[gh],
                                    bank[BK_DS][:, h * 128:(h + 1) * 128],
                                    ALU.mult, ALU.add),
                                deps=[bk_w[BK_DS], dsbt,
                                      scan_state.get("sb2_tok")], inc="V")
                        bk_r[BK_DS] = up_tok
                        sb2 = None
                        for h in range(HG):
                            gh = hg * HG + h
                            sb2 = p.add("scalar", lambda ss, par=par, gh=gh, h=h:
                                        ss.mul(Sb[par][:, h * 128:(h + 1) * 128],
                                               Sst[:, gh * 128:(gh + 1) * 128],
                                               float(lamC[gh])),
                                        deps=[up_tok, bk_w[BK_O0 + par]],
                                        inc="S")
                        scan_state["sb2_tok"] = sb2
                    xrt = evac("vector", BK_O0 + par, lambda v, par=par, c=c:
                               v.tensor_mul(xr[par][:, :],
                                            bank[BK_O0 + par][:, 0:256],
                                            sgm[:, c * 256:(c + 1) * 256]),
                               deps=[sgm_tok[c]])
                    scan_state["dsb_rd"] = bk_w[BK_O0 + par]
                    rs_tok = []
                    for h in range(HG):
                        msh = col()
                        t1 = p.add("scalar", lambda ss, par=par, h=h, msh=msh:
                                   ss.activation(dum2[:, h * 128:(h + 1) * 128],
                                                 xr[par][:, h * 128:(h + 1) * 128],
                                                 AF.Square, accum_out=msh),
                                   deps=[xrt], inc="S")
                        sdh = col()
                        t2 = p.add("scalar", lambda ss, msh=msh, sdh=sdh:
                                   ss.activation(sdh, msh, AF.Sqrt,
                                                 scale=np.float32(1.0 / HD), bias=np.float32(EPS_RMS)),
                                   deps=[t1], inc="S")
                        rsh = col()
                        t3 = p.add("vector", lambda v, sdh=sdh, rsh=rsh:
                                   v.reciprocal(rsh, sdh), deps=[t2], inc="V")
                        rs_tok.append((t3, rsh))
                    opn_tok = None
                    for h in range(HG):
                        t3, rsh = rs_tok[h]
                        opn_tok = p.add("vector", lambda v, par=par, h=h, rsh=rsh:
                                        v.tensor_scalar(
                                            opn[par][:, h * 128:(h + 1) * 128],
                                            xr[par][:, h * 128:(h + 1) * 128],
                                            rsh, None, ALU.mult),
                                        deps=[t3, opn_free[par]], inc="V")

                    def make_otr(par=par, hg=hg, Hh=Hh, c=c, opn_tok=opn_tok):
                        def go():
                            parts = []
                            for h in range(HG):
                                parts.append((bank[BK_OT][:, h * 128:(h + 1) * 128],
                                              opn[par][:, h * 128:(h + 1) * 128],
                                              ident[:, :], True, True))
                            mm_round(BK_OT, parts, [opn_tok])
                            opn_free[par] = bk_w[BK_OT]
                            sp_tok = None
                            for h in range(HG):
                                gh = hg * HG + h
                                sp_tok = evac(
                                    "vector", BK_OT,
                                    lambda v, par=par, h=h, gh=gh:
                                    v.tensor_scalar(
                                        osp[par][:, h * 128:(h + 1) * 128],
                                        bank[BK_OT][:, h * 128:(h + 1) * 128],
                                        nwc[:, gh:gh + 1], None, ALU.mult),
                                    deps=[osp_dma[par]] if h == 0 else ())
                            dst = s_o[Hh * 8 + c, :,
                                      hg * 256:(hg + 1) * 256]
                            srcv = osp[par][:, :]
                            osp_dma[par] = p.add(
                                "scalar", lambda ss, dst=dst, srcv=srcv:
                                ss.dma_start(out=dst, in_=srcv),
                                deps=[sp_tok], inc="DS", amt=16)
                            if hg == NHG - 1:
                                scan_state.setdefault("osp_grid", {})[
                                    (Hh, c)] = osp_dma[par]
                        return go
                    pend_otrs.append(make_otr())
                    last_v_mm = bk_w[BK_V]
                    snaps.append([p.last.get("T"), p.last.get("V"),
                                  p.last.get("S")])
                wbig_free[0] = last_v_mm
                wbig_free[1] = last_v_mm
                scan_state["scan_last"] = p.last["T"]
            while pend_otrs:
                pend_otrs.pop(0)()
            scan_state["scan_last"] = p.last["T"]
            return

        def phase_wo(s, osp_grid):
            l2_tok = [None] * NCH
            x2_tok = [None] * NCH
            mgt_tok = [None] * NCH
            strip_tok = [None, None]
            pend_l2 = []
            lnbf_rd = [None, None, None]
            g8_prev = [None]
            for t in range(NCH):
                prt = t % 2
                lb = t % 3
                stok = p.add("sync", lambda y, t=t, prt=prt: y.dma_start(
                    out=oTb[prt][:, :], in_=s_o[t, :, :]),
                    deps=[oTb_free[prt], osp_grid[(t // 8, t % 8)]],
                    inc="DX", amt=16)
                xtok = p.add("gpsimd", lambda g, s=s, t=t, prt=prt: g.dma_start(
                    out=xt[prt][:, :], in_=dx[s, t * 128:(t + 1) * 128, :]),
                    deps=[xt_free[prt]], inc="DW", amt=16)
                pcols = []
                last_mm = None
                for ntt in range(4):
                    bki = BK_S0 if ntt % 2 == 0 else BK_S1
                    parts = []
                    for k in range(KT):
                        parts.append((bank[bki][:, :],
                                      oTb[prt][:, k * 128:(k + 1) * 128],
                                      WB[:, k * D + ntt * 512: k * D + ntt * 512 + 512],
                                      k == 0, k == KT - 1))
                    last_mm = mm_round(bki, parts, [stok, wb_tok[0]])
                    pc = col()
                    pcols.append(pc)
                    evac("vector", bki, lambda v, bki=bki, prt=prt, ntt=ntt, pc=pc:
                         v.tensor_scalar(ao[prt][:, ntt * 512:(ntt + 1) * 512],
                                         bank[bki][:, :], 1.0, 0.0, ALU.mult,
                                         ALU.add, accum_out=pc),
                         deps=[ao_free[prt]])
                oTb_free[prt] = last_mm
                ev4 = p.last["V"]
                if len(pend_l2) >= 2:
                    pend_l2.pop(0)()
                s01 = col()
                a1 = p.add("vector", lambda v, a=pcols[0], b=pcols[1], o=s01:
                           v.tensor_add(o, a, b), deps=[ev4], inc="V")
                s23 = col()
                a2 = p.add("vector", lambda v, a=pcols[2], b=pcols[3], o=s23:
                           v.tensor_add(o, a, b), deps=[ev4], inc="V")
                sxa = col()
                a3 = p.add("vector", lambda v, a=s01, b=s23, o=sxa:
                           v.tensor_add(o, a, b), deps=[a1, a2], inc="V")
                sqa = col()
                sq1 = p.add("scalar", lambda ss, prt=prt, sqa=sqa: ss.activation(
                    snk[:, :], ao[prt][:, :], AF.Square, accum_out=sqa),
                    deps=[ev4], inc="S")
                m = col()
                b1 = p.add("vector", lambda v, sxa=sxa, m=m: v.tensor_scalar(
                    m, sxa, 1.0 / D, 0.0, ALU.mult, ALU.add), deps=[a3], inc="V")
                msq = col()
                b2 = p.add("vector", lambda v, m=m, msq=msq: v.tensor_scalar(
                    msq, m, m, None, ALU.mult), deps=[b1], inc="V")
                var = col()
                b3 = p.add("vector", lambda v, sqa=sqa, msq=msq, var=var:
                           v.scalar_tensor_tensor(var, sqa, 1.0 / D, msq,
                                                  ALU.mult, ALU.subtract),
                           deps=[sq1, b2], inc="V")
                sd = col()
                b4 = p.add("scalar", lambda ss, var=var, sd=sd: ss.activation(
                    sd, var, AF.Sqrt, bias=np.float32(EPS_LN)), deps=[b3], inc="S")
                rs = col()
                b5 = p.add("vector", lambda v, sd=sd, rs=rs: v.reciprocal(rs, sd),
                           deps=[b4], inc="V")
                z1 = p.add("vector", lambda v, prt=prt, m=m, rs=rs:
                           v.tensor_scalar(ao[prt][:, :], ao[prt][:, :], m, rs,
                                           ALU.subtract, ALU.mult),
                           deps=[b5, sq1], inc="V")
                sx2 = col()
                z2 = p.add("vector", lambda v, prt=prt, sx2=sx2:
                           v.scalar_tensor_tensor(ao[prt][:, :], ao[prt][:, :],
                                                  1.0, xt[prt][:, :], ALU.mult,
                                                  ALU.add, accum_out=sx2),
                           deps=[z1, xtok], inc="V")
                xt_free[prt] = z2
                x2_tok[t] = p.add("scalar", lambda ss, t=t, prt=prt: ss.dma_start(
                    out=s_x2[t * 128:(t + 1) * 128, :], in_=ao[prt][:, :]),
                    deps=[z2], inc="DS", amt=16)
                sq2c = col()
                c1 = p.add("scalar", lambda ss, prt=prt, sq2c=sq2c: ss.activation(
                    snk[:, :], ao[prt][:, :], AF.Square, accum_out=sq2c),
                    deps=[z2], inc="S")
                m2 = col()
                c2 = p.add("vector", lambda v, sx2=sx2, m2=m2: v.tensor_scalar(
                    m2, sx2, 1.0 / D, 0.0, ALU.mult, ALU.add), deps=[z2], inc="V")
                m2q = col()
                c3 = p.add("vector", lambda v, m2=m2, m2q=m2q: v.tensor_scalar(
                    m2q, m2, m2, None, ALU.mult), deps=[c2], inc="V")
                v2r = col()
                c4 = p.add("vector", lambda v, sq2c=sq2c, m2q=m2q, v2r=v2r:
                           v.scalar_tensor_tensor(v2r, sq2c, 1.0 / D, m2q,
                                                  ALU.mult, ALU.subtract),
                           deps=[c1, c3], inc="V")
                sd2 = col()
                c5 = p.add("scalar", lambda ss, v2r=v2r, sd2=sd2: ss.activation(
                    sd2, v2r, AF.Sqrt, bias=np.float32(EPS_LN)), deps=[c4], inc="S")
                rs2 = col()
                c6 = p.add("vector", lambda v, sd2=sd2, rs2=rs2: v.reciprocal(
                    rs2, sd2), deps=[c5], inc="V")
                cap = p.add("vector", lambda v, lb=lb, prt=prt, m2=m2, rs2=rs2:
                            v.tensor_scalar(lnbf[lb][:, :], ao[prt][:, :], m2,
                                            rs2, ALU.subtract, ALU.mult),
                            deps=[c6, c1, lnbf_rd[lb]], inc="V")
                ao_free[prt] = cap

                def make_l2tr(t=t, prt=prt, lb=lb, cap=cap):
                    def go():
                        l2b = qT if prt == 0 else kT
                        ev_t = None
                        lr = None
                        for r in range(4):
                            bki = BK_O0 if r % 2 == 0 else BK_O1
                            parts = []
                            for j in range(4):
                                kk = 4 * r + j
                                parts.append((bank[bki][:, j * 128:(j + 1) * 128],
                                              lnbf[lb][:, kk * 128:(kk + 1) * 128],
                                              ident[:, :], True, True))
                            lr = mm_round(bki, parts, [cap])
                            dst = l2b[:, :].rearrange("p (k n) -> p k n", k=KT)[
                                :, 4 * r:4 * r + 4, 0:128]
                            ev_t = evac("scalar", bki, lambda ss, bki=bki, dst=dst:
                                        ss.copy(dst, bank[bki][:, :].rearrange(
                                            "p (k n) -> p k n", k=4)),
                                        deps=[strip_tok[prt]])
                        dt_ = s_l2[t, :, :]
                        l2_tok[t] = p.add("scalar", lambda ss, l2b=l2b, dt_=dt_:
                                          ss.dma_start(out=dt_, in_=l2b[:, :]),
                                          deps=[ev_t], inc="DS", amt=16)
                        lnbf_rd[lb] = lr
                        parts = []
                        for k in range(KT):
                            parts.append((bank[BK_V][:, 0:136],
                                          l2b[:, k * 128:(k + 1) * 128],
                                          arc[:, k * 136:(k + 1) * 136],
                                          k == 0, k == KT - 1))
                        rmm = mm_round(BK_V, parts, [ev_t, arc_tok[0]])
                        strip_tok[prt] = [l2_tok[t], rmm]
                        lgt = p.add("vector", lambda v, t=t: v.tensor_tensor(
                            lgall[:, t * 8:(t + 1) * 8],
                            bank[BK_V][:, 128:136], rbs[:, :],
                            ALU.add), deps=[bk_w[BK_V], rb_tok[0]], inc="V")
                        msh = p.add("scalar", lambda ss, t=t: ss.copy(
                            mid16[:, t * 128:(t + 1) * 128],
                            bank[BK_V][:, 0:128]), deps=[bk_w[BK_V], lgt],
                            inc="S")
                        bk_r[BK_V] = msh
                    return go
                pend_l2.append(make_l2tr())
            wo_done = p.last["T"]
            for ntt in range(4):
                wbase_tok[ntt] = p.add("gpsimd", lambda g, ntt=ntt: g.dma_start(
                    out=WB[:, :].rearrange("p (k n) -> p k n", k=KT)[
                        :, :, ntt * 512:(ntt + 1) * 512],
                    in_=dBWT.rearrange("(k p) n -> p k n", p=128)[
                        :, :, ntt * 512:(ntt + 1) * 512]),
                    deps=[wb_tok[0], wo_done], inc="DP", amt=16)
            while pend_l2:
                pend_l2.pop(0)()
            mde_free = [None, None]
            for t in range(NCH):
                lgs = lgall[:, t * 8:(t + 1) * 8]
                m1 = col()
                g2 = p.add("vector", lambda v, lgs=lgs, m1=m1: v.reduce_max(
                    m1, lgs, axis=AXX), deps=[g8_prev[0]], inc="V")
                g3 = p.add("vector", lambda v, lgs=lgs, m1=m1: v.tensor_scalar(
                    g8a[:, :], lgs, m1, None, ALU.is_ge), deps=[g2], inc="V")
                g4 = p.add("vector", lambda v, lgs=lgs, m1=m1: v.tensor_scalar(
                    lgs, lgs, m1, None, ALU.subtract), deps=[g3], inc="V")
                g5 = p.add("vector", lambda v, lgs=lgs: v.scalar_tensor_tensor(
                    g8b[:, :], g8a[:, :], -1e30, lgs, ALU.mult, ALU.add),
                    deps=[g4], inc="V")
                m2g = col()
                g6 = p.add("vector", lambda v, m2g=m2g: v.reduce_max(
                    m2g, g8b[:, :], axis=AXX), deps=[g5], inc="V")
                e2 = col()
                g7 = p.add("scalar", lambda ss, m2g=m2g, e2=e2: ss.activation(
                    e2, m2g, AF.Exp), deps=[g6], inc="S")
                g8 = p.add("scalar", lambda ss, lgs=lgs: ss.activation(
                    g8a[:, :], lgs, AF.Exp), deps=[g4, g5], inc="S")
                g8_prev[0] = g8
                den = col()
                g9 = p.add("vector", lambda v, e2=e2, den=den: v.tensor_scalar(
                    den, e2, 1.0, None, ALU.add), deps=[g7], inc="V")
                rde = col()
                gA = p.add("vector", lambda v, den=den, rde=rde: v.reciprocal(
                    rde, den), deps=[g9], inc="V")
                gB = p.add("vector", lambda v, lgs=lgs, m2g=m2g: v.tensor_scalar(
                    g8b[:, :], lgs, m2g, None, ALU.is_ge), deps=[g6, g8],
                    inc="V")
                gC = p.add("vector", lambda v: v.tensor_mul(
                    g8a[:, :], g8a[:, :], g8b[:, :]), deps=[g8, gB], inc="V")
                gD = p.add("vector", lambda v, rde=rde: v.tensor_scalar(
                    gat[:, :], g8a[:, :], rde, 2.0, ALU.mult, ALU.mult),
                    deps=[gA, gC], inc="V")
                gbc = gat[:, :].broadcast_to((128, 8, 16))
                mid3 = mid16[:, t * 128:(t + 1) * 128].rearrange(
                    "p (e r) -> p e r", e=8)
                out3 = mde2[t % 2][:, :].rearrange("p (e r) -> p e r", e=8)
                me = p.add("vector", lambda v, gbc=gbc, mid3=mid3, out3=out3:
                           v.tensor_tensor(out3, mid3, gbc, ALU.mult),
                           deps=[gD, mde_free[t % 2]], inc="V")
                bkt = BK_KT if t % 2 == 0 else BK_DS
                parts = [(bank[bkt][:, 0:128], mde2[t % 2][:, :],
                          ident[:, :], True, True)]
                mm_round(bkt, parts, [me])
                mde_free[t % 2] = bk_w[bkt]
                mgt_tok[t] = evac("vector", bkt, lambda v, t=t, bkt=bkt:
                                  v.tensor_copy(mgt[:, t * 128:(t + 1) * 128],
                                                bank[bkt][:, 0:128]))
            return l2_tok, x2_tok, mgt_tok

        def phase_ffn(s, l2_tok, x2_tok, mgt_tok):
            for t in range(NCH):
                prt = t % 2
                stok = p.add("sync", lambda y, t=t, prt=prt: y.dma_start(
                    out=oTb[prt][:, :], in_=s_l2[t, :, :]),
                    deps=[oTb_free[prt], l2_tok[t]], inc="DX", amt=16)
                xtok = p.add("gpsimd", lambda g, t=t, prt=prt: g.dma_start(
                    out=xt[prt][:, :], in_=s_x2[t * 128:(t + 1) * 128, :]),
                    deps=[xt_free[prt], x2_tok[t]], inc="DW", amt=16)
                pcols = []
                last_mm = None
                for ntt in range(4):
                    bki = [BK_S0, BK_S1, BK_O0, BK_O1][ntt]
                    parts = []
                    parts.append((bank[bki][:, :],
                                  oTb[prt][:, 0:128],
                                  WB[:, 0 * D + ntt * 512: 0 * D + ntt * 512 + 512],
                                  True, False))
                    parts.append((bank[bki][:, :], onesr[0:1, :],
                                  bbr[0:1, ntt * 512:(ntt + 1) * 512],
                                  False, False))
                    for k in range(1, KT):
                        parts.append((bank[bki][:, :],
                                      oTb[prt][:, k * 128:(k + 1) * 128],
                                      WB[:, k * D + ntt * 512: k * D + ntt * 512 + 512],
                                      False, False))
                    parts.append((bank[bki][:, :],
                                  mgt[:, t * 128:(t + 1) * 128],
                                  bcat[:, ntt * 512:(ntt + 1) * 512],
                                  False, True))
                    last_mm = mm_round(bki, parts,
                                       [stok, wbase_tok[3], bb_tok[0],
                                        bc_tok[0], mgt_tok[t]])
                    pc = col()
                    pcols.append(pc)
                    evac("vector", bki, lambda v, bki=bki, prt=prt, ntt=ntt, pc=pc:
                         v.tensor_scalar(ao[prt][:, ntt * 512:(ntt + 1) * 512],
                                         bank[bki][:, :], 1.0, 0.0, ALU.mult,
                                         ALU.add, accum_out=pc),
                         deps=[ao_free[prt]])
                oTb_free[prt] = last_mm
                ev4 = p.last["V"]
                s01 = col()
                a1 = p.add("vector", lambda v, a=pcols[0], b=pcols[1], o=s01:
                           v.tensor_add(o, a, b), deps=[ev4], inc="V")
                s23 = col()
                a2 = p.add("vector", lambda v, a=pcols[2], b=pcols[3], o=s23:
                           v.tensor_add(o, a, b), deps=[ev4], inc="V")
                sxa = col()
                a3 = p.add("vector", lambda v, a=s01, b=s23, o=sxa:
                           v.tensor_add(o, a, b), deps=[a1, a2], inc="V")
                sqa = col()
                sq1 = p.add("scalar", lambda ss, prt=prt, sqa=sqa: ss.activation(
                    snk[:, :], ao[prt][:, :], AF.Square, accum_out=sqa),
                    deps=[ev4], inc="S")
                m = col()
                b1 = p.add("vector", lambda v, sxa=sxa, m=m: v.tensor_scalar(
                    m, sxa, 1.0 / D, 0.0, ALU.mult, ALU.add), deps=[a3], inc="V")
                msq = col()
                b2 = p.add("vector", lambda v, m=m, msq=msq: v.tensor_scalar(
                    msq, m, m, None, ALU.mult), deps=[b1], inc="V")
                var = col()
                b3 = p.add("vector", lambda v, sqa=sqa, msq=msq, var=var:
                           v.scalar_tensor_tensor(var, sqa, 1.0 / D, msq,
                                                  ALU.mult, ALU.subtract),
                           deps=[sq1, b2], inc="V")
                sd = col()
                b4 = p.add("scalar", lambda ss, var=var, sd=sd: ss.activation(
                    sd, var, AF.Sqrt, bias=np.float32(EPS_LN)), deps=[b3], inc="S")
                rs = col()
                b5 = p.add("vector", lambda v, sd=sd, rs=rs: v.reciprocal(rs, sd),
                           deps=[b4], inc="V")
                z1 = p.add("vector", lambda v, prt=prt, m=m, rs=rs:
                           v.tensor_scalar(ao[prt][:, :], ao[prt][:, :], m, rs,
                                           ALU.subtract, ALU.mult),
                           deps=[b5, sq1], inc="V")
                z2 = p.add("vector", lambda v, prt=prt:
                           v.scalar_tensor_tensor(ao[prt][:, :], ao[prt][:, :],
                                                  1.0, xt[prt][:, :], ALU.mult,
                                                  ALU.add),
                           deps=[z1, xtok], inc="V")
                xt_free[prt] = z2
                otok = p.add("scalar", lambda ss, s=s, t=t, prt=prt: ss.dma_start(
                    out=dout[s, t * 128:(t + 1) * 128, :], in_=ao[prt][:, :]),
                    deps=[z2], inc="DS", amt=16)
                out_dma_count[0] += 1
                ao_free[prt] = otok

        wb_tok = [None, None]
        wbase_tok = [None] * 4
        load_gate_weights(None)

        def reset_free(slot):
            fz = frees()
            for i in range(2):
                old = slot[i]
                slot[i] = (old if isinstance(old, list) else [old]) + fz

        for s in range(NSEQ):
            p.barrier()
            for slot in (xt_free, ao_free, oTb_free, wbig_free):
                reset_free(slot)
            tab_tok = load_tables(frees() + list(ao_free[0]) + list(ao_free[1]))
            zt = p.add("vector", lambda v: v.memset(Sst, 0.0),
                       deps=frees() + list(ao_free[0]), inc="V")
            scan_state = {"sb_tok": None, "scan_last": None,
                          "rope_prev": None, "rope_last": None,
                          "tab_tok": tab_tok}
            for Hh in range(2):
                scan_state["p0_done"] = phase_p0(
                    s, Hh, first_p0=(s == 0 and Hh == 0))
                if Hh == 0:
                    tl = None
                    for kk in range(4):
                        tl = p.add("gpsimd", lambda g, kk=kk: g.dma_start(
                            out=WB[:, :].rearrange("p (k n) -> p k n", k=KT)[
                                :, 4 * kk:4 * kk + 4, :],
                            in_=dWo[:, 4 * kk:4 * kk + 4, :]),
                            deps=[p.last.get("T")], inc="DW", amt=16)
                    wb_tok[0] = tl
                phase_attn(s, Hh, scan_state)
            for slot in (xt_free, ao_free, oTb_free):
                reset_free(slot)
            load_ffn_weights(frees())
            l2_tok, x2_tok, mgt_tok = phase_wo(s, scan_state["osp_grid"])
            p.barrier()
            phase_ffn(s, l2_tok, x2_tok, mgt_tok)
            p.barrier()

        p.add("scalar", lambda ss: ss.nop(), deps=[("DS", p.n["DS"])])

        block = es.enter_context(nc.Block())

        def emit(eng_name):
            def body(e):
                for fn, ws, inc, amt in p.ops[eng_name]:
                    for sname, cval in ws:
                        e.wait_ge(sems[sname], cval)
                    if fn is None:
                        continue
                    ins = fn(e)
                    if inc is not None and ins is not None:
                        ins.then_inc(sems[inc], amt)
            return body

        block.tensor(emit("tensor"))
        block.vector(emit("vector"))
        block.scalar(emit("scalar"))
        block.gpsimd(emit("gpsimd"))
        block.sync(emit("sync"))

    return nc


def _consts():
    lam = -(2.0 ** (-8.0 * (np.arange(H, dtype=np.float64) + 1.0) / H))
    i = np.arange(C, dtype=np.float64)
    DTm = np.zeros((128, H * C), np.float64)
    DQm = np.zeros((128, H * C), np.float64)
    dkc = np.zeros((128, H), np.float64)
    for h in range(H):
        diff = i[None, :] - i[:, None]
        DTm[:, h * C:(h + 1) * C] = np.where(diff >= 0, np.exp(lam[h] * diff), 0.0)
        DQm[:, h * C:(h + 1) * C] = np.exp(lam[h] * (i[None, :] + 1.0))
        dkc[:, h] = np.exp(lam[h] * (C - 1.0 - i))
    half = HD // 2
    theta = 10000.0 ** (-2.0 * np.arange(half, dtype=np.float64) / HD)
    ang = np.arange(T, dtype=np.float64)[None, :] * \
        np.concatenate([theta, theta])[:, None]
    cosF = np.cos(ang)
    sinF = np.sin(ang)
    return {
        "DTm": DTm.astype(bf), "DQm": DQm.astype(bf),
        "dkc": dkc.astype(np.float32),
        "cosF": cosF.astype(bf), "sinF": sinF.astype(bf),
        "ident": np.eye(128, dtype=bf),
        "onesr": np.ones((1, 128), dtype=bf),
    }


_NC_CACHE = {}


def kernel(L0_attn_x, L0_attn_Wq, L0_attn_Wk, L0_attn_Wv, L0_attn_Wo,
           L0_attn_Wg1, L0_attn_Wg2, L0_attn_norm_weight,
           L0_ffn_base_w, L0_ffn_base_b, L0_ffn_router_w, L0_ffn_router_b,
           L0_ffn_lora_A, L0_ffn_lora_B):
    x = np.ascontiguousarray(np.asarray(L0_attn_x, dtype=np.float32))
    cst = _consts()
    shared = {
        "WqS": np.ascontiguousarray(
            np.asarray(L0_attn_Wq, dtype=np.float32)
            .reshape(16, 128, 16, 128).transpose(1, 2, 0, 3)),
        "WkS": np.ascontiguousarray(
            np.asarray(L0_attn_Wk, dtype=np.float32)
            .reshape(16, 128, 16, 128).transpose(1, 2, 0, 3)),
        "WvS": np.ascontiguousarray(
            np.asarray(L0_attn_Wv, dtype=np.float32)
            .reshape(16, 128, 8, 256).transpose(1, 2, 0, 3)),
        "WoS": np.ascontiguousarray(
            np.asarray(L0_attn_Wo, dtype=np.float32)
            .reshape(16, 128, 2048).transpose(1, 0, 2)),
        "Wg1": np.ascontiguousarray(L0_attn_Wg1, dtype=np.float32),
        "Wg2": np.ascontiguousarray(L0_attn_Wg2, dtype=np.float32),
        "baseWT": np.ascontiguousarray(
            np.asarray(L0_ffn_base_w, dtype=np.float32).T),
        "ARcat": np.ascontiguousarray(np.concatenate([
            np.asarray(L0_ffn_lora_A, dtype=np.float32)
            .transpose(1, 0, 2).reshape(D, 128),
            np.asarray(L0_ffn_router_w, dtype=np.float32).T], axis=1)),
        "Bcat": np.ascontiguousarray(
            np.asarray(L0_ffn_lora_B, dtype=np.float32).reshape(128, D)),
        "baseb": np.ascontiguousarray(
            np.asarray(L0_ffn_base_b, dtype=np.float32).reshape(1, D)),
        "rbb": np.ascontiguousarray(np.tile(
            np.asarray(L0_ffn_router_b, dtype=np.float32).reshape(1, 8),
            (128, 1))),
        "nwT": np.ascontiguousarray(
            np.asarray(L0_attn_norm_weight, dtype=np.float32)
            .reshape(16, 128).T),
        "ident": cst["ident"], "cosF": cst["cosF"], "sinF": cst["sinF"],
        "DTm": cst["DTm"], "DQm": cst["DQm"], "dkc": cst["dkc"],
        "onesr": cst["onesr"],
    }
    nc = build_program()
    in_maps = []
    for c in range(8):
        m = dict(shared)
        m["x"] = np.ascontiguousarray(x[c * NSEQ:(c + 1) * NSEQ])
        in_maps.append(m)
    res = run_bass_kernel_spmd(nc, in_maps, core_ids=list(range(8)))
    return np.concatenate([r["out"] for r in res.results], axis=0)


if __name__ == "__main__":
    import time
    t0 = time.time()
    nc = build_program()
    print("built", nc.n_instructions(), "instructions in", time.time() - t0, "s")


# version 31
# speedup vs baseline: 1.2059x; 1.2059x over previous
import numpy as np
import ml_dtypes
import concourse.bass as bass
import concourse.mybir as mybir
from concourse.bass_utils import run_bass_kernel_spmd

F32 = mybir.dt.float32
BF16 = mybir.dt.bfloat16
AF = mybir.ActivationFunctionType
ALU = mybir.AluOpType
AXX = mybir.AxisListType.X
bf = ml_dtypes.bfloat16

D = 2048
T = 2048
NSEQ = 2
H = 16
HD = 128
C = 128
NCH = 16
HG = 2
NHG = H // HG
TH = 1024
KT = 16
EPS_LN = 1e-5
EPS_RMS = 1e-6

ENGINES = ("tensor", "vector", "scalar", "gpsimd", "sync")


class Prog:

    def __init__(self):
        self.ops = {e: [] for e in ENGINES}
        self.n = {}
        self.hw = {}
        self.last = {}

    def sem(self, name):
        self.n[name] = 0

    def add(self, eng, fn, deps=(), inc=None, amt=1):
        ws = []
        flat = []
        for d in deps:
            if isinstance(d, list):
                flat.extend(d)
            else:
                flat.append(d)
        for d in flat:
            if d is None:
                continue
            s, c = d
            if c <= 0:
                continue
            k = (eng, s)
            if c > self.hw.get(k, 0):
                self.hw[k] = c
                ws.append((s, c))
        tok = None
        if inc is not None:
            self.n[inc] += amt
            tok = (inc, self.n[inc])
            self.last[inc] = tok
        self.ops[eng].append((fn, ws, inc, amt))
        return tok

    def barrier(self, engines=("tensor", "vector", "scalar")):
        toks = [self.last.get(s) for s in ("T", "V", "S")]
        for e in engines:
            self.add(e, None, deps=toks)


def build_program(debug=False):
    nc = bass.Bass("TRN2", target_bir_lowering=False)

    dx = nc.dram_tensor("x", [NSEQ, T, D], F32, kind="ExternalInput").ap()
    dout = nc.dram_tensor("out", [NSEQ, T, D], F32, kind="ExternalOutput").ap()
    dWq = nc.dram_tensor("WqS", [128, H, KT, 128], F32, kind="ExternalInput").ap()
    dWk = nc.dram_tensor("WkS", [128, H, KT, 128], F32, kind="ExternalInput").ap()
    dWv = nc.dram_tensor("WvS", [128, NHG, KT, 256], F32, kind="ExternalInput").ap()
    dWo = nc.dram_tensor("WoS", [128, KT, D], F32, kind="ExternalInput").ap()
    dWg1 = nc.dram_tensor("Wg1", [D, HD], F32, kind="ExternalInput").ap()
    dWg2 = nc.dram_tensor("Wg2", [HD, D], F32, kind="ExternalInput").ap()
    dBWT = nc.dram_tensor("baseWT", [D, D], F32, kind="ExternalInput").ap()
    dARC = nc.dram_tensor("ARcat", [D, 136], F32, kind="ExternalInput").ap()
    dBC = nc.dram_tensor("Bcat", [128, D], F32, kind="ExternalInput").ap()
    dBB = nc.dram_tensor("baseb", [1, D], F32, kind="ExternalInput").ap()
    dRB = nc.dram_tensor("rbb", [128, 8], F32, kind="ExternalInput").ap()
    dONE = nc.dram_tensor("onesr", [1, 128], BF16, kind="ExternalInput").ap()
    dNW = nc.dram_tensor("nwT", [128, 16], F32, kind="ExternalInput").ap()
    dID = nc.dram_tensor("ident", [128, 128], BF16, kind="ExternalInput").ap()
    dCOS = nc.dram_tensor("cosF", [128, T], BF16, kind="ExternalInput").ap()
    dSIN = nc.dram_tensor("sinF", [128, T], BF16, kind="ExternalInput").ap()
    dDT = nc.dram_tensor("DTm", [128, H * C], BF16, kind="ExternalInput").ap()
    dDQ = nc.dram_tensor("DQm", [128, H * C], BF16, kind="ExternalInput").ap()
    dDK = nc.dram_tensor("dkc", [128, H], F32, kind="ExternalInput").ap()
    s_x2 = nc.dram_tensor("scr_x2", [T, D], F32, kind="Internal").ap()
    s_o = nc.dram_tensor("scr_o", [NCH, 128, KT * 128], BF16, kind="Internal").ap()
    s_l2 = nc.dram_tensor("scr_l2", [NCH, 128, KT * 128], BF16, kind="Internal").ap()

    lam = -(2.0 ** (-8.0 * (np.arange(H, dtype=np.float64) + 1.0) / H))
    lamC = [float(np.exp(l * C)) for l in lam]

    import contextlib
    es = contextlib.ExitStack()
    p = Prog()
    for s in ("T", "V", "S", "DX", "DW", "DS"):
        p.sem(s)

    with es:
        sbuf = lambda n, sh, dt: es.enter_context(nc.sbuf_tensor(n, sh, dt))
        lnT = sbuf("lnT", [128, KT * TH], BF16)
        WB = sbuf("WB", [128, KT * D], BF16)
        xt = [sbuf(f"xt{i}", [128, D], F32) for i in range(2)]
        lnbf = [sbuf(f"lnbf{i}", [128, D], BF16) for i in range(3)]
        qT = sbuf("qT", [128, HG * TH], BF16)
        kT = sbuf("kT", [128, HG * TH], BF16)
        dkc = sbuf("dkcs", [128, H], F32)
        nwc = sbuf("nwcs", [128, 16], F32)
        ident = sbuf("idents", [128, 128], BF16)
        onesr = sbuf("onesrs", [1, 128], BF16)
        rbs = sbuf("rbs", [128, 8], F32)
        wpf = sbuf("wpf", [128, 2048], BF16)
        cols = sbuf("cols", [128, 48], F32)
        lg8 = sbuf("lg8", [128, 8], F32)
        g8a = sbuf("g8a", [128, 8], F32)
        g8b = sbuf("g8b", [128, 8], F32)
        gat = sbuf("gat", [128, 8], F32)
        univ = sbuf("univ", [128, 14096], F32)
        ub = univ.bitcast(BF16)
        Sst = univ[:, 0:2048]
        xr = [univ[:, 2048 + 256 * i: 2048 + 256 * (i + 1)] for i in range(2)]
        wbig = ub[:, 5120:9216]
        st0 = [ub[:, 9216 + 512 * i: 9216 + 512 * (i + 1)] for i in range(2)]
        rt1 = ub[:, 10240:10752]
        rt2 = ub[:, 10752:11264]
        vb = [ub[:, 11264 + 256 * i: 11264 + 256 * (i + 1)] for i in range(2)]
        scm = [ub[:, 11776 + 256 * i: 11776 + 256 * (i + 1)] for i in range(2)]
        qdq = [ub[:, 12288 + 256 * i: 12288 + 256 * (i + 1)] for i in range(2)]
        kdk = [ub[:, 12800 + 256 * i: 12800 + 256 * (i + 1)] for i in range(2)]
        Sb = [ub[:, 13312 + 256 * i: 13312 + 256 * (i + 1)] for i in range(2)]
        opn = [ub[:, 13824 + 256 * i: 13824 + 256 * (i + 1)] for i in range(2)]
        osp = [ub[:, 14336 + 256 * i: 14336 + 256 * (i + 1)] for i in range(2)]
        dum2 = ub[:, 14848:15104]
        cosH = ub[:, 15104:16128]
        sinH = ub[:, 16128:17152]
        DTm = ub[:, 17152:19200]
        DQm = ub[:, 19200:21248]
        midT = ub[:, 21248:23296]
        wg1 = ub[:, 23296:25344]
        wg2 = ub[:, 25344:27392]
        dSb = [ub[:, 27424 + 256 * i: 27424 + 256 * (i + 1)] for i in range(2)]
        Sbsd = ub[:, 27936:28192]
        ao = [univ[:, 2048 * i: 2048 * (i + 1)] for i in range(2)]
        oTb = [ub[:, 8192 + 2048 * i: 8192 + 2048 * (i + 1)] for i in range(2)]
        arc = ub[:, 12544:14720]
        p0snk = ub[:, 12544:14592]
        bcat = ub[:, 14720:16768]
        bbr = ub[0:1, 16768:18816]
        snk = ub[:, 19200:21248]
        mgt = ub[:, 21248:23296]
        mid16 = mgt
        lgall = univ[:, 13712:13840]
        mde2 = [ub[:, 27936 + 128 * i: 27936 + 128 * (i + 1)] for i in range(2)]
        sgm = xt[0].bitcast(BF16)

        bank = [es.enter_context(nc.psum_tensor(f"bk{i}", [128, 512], F32))
                for i in range(8)]
        BK_V, BK_S0, BK_S1, BK_O0, BK_O1, BK_DS, BK_KT, BK_OT = range(8)

        sems = {}
        for s in ("T", "V", "S", "DX", "DW", "DS"):
            sems[s] = es.enter_context(nc.semaphore(s))

        col_i = [0]

        def col():
            col_i[0] = (col_i[0] + 1) % 48
            return cols[:, col_i[0]:col_i[0] + 1]

        ldtok = []
        for dst, srcd in ((ident, dID), (dkc, dDK), (nwc, dNW), (onesr, dONE)):
            ldtok.append(p.add("sync", lambda y, d=dst, s=srcd: y.dma_start(
                out=d[:, :], in_=s), inc="DX", amt=16))

        def load_tables(dep):
            t1 = p.add("sync", lambda y: y.dma_start(out=DTm, in_=dDT),
                       deps=dep, inc="DX", amt=16)
            t2 = p.add("sync", lambda y: y.dma_start(out=DQm, in_=dDQ),
                       deps=dep, inc="DX", amt=16)
            return [t1, t2]
        consts_tok = ldtok
        wtok = [None, None]
        arc_tok = [None]; bc_tok = [None]; bb_tok = [None]; rb_tok = [None]

        def load_gate_weights(dep):
            wtok[0] = p.add("gpsimd", lambda g: g.dma_start(
                out=wg1.rearrange("p (k n) -> p k n", k=KT),
                in_=dWg1.rearrange("(k p) n -> p k n", p=128)),
                deps=[dep], inc="DW", amt=16)
            wtok[1] = p.add("gpsimd", lambda g: g.dma_start(
                out=wg2, in_=dWg2), deps=[dep], inc="DW", amt=16)

        rb_tok[0] = p.add("gpsimd", lambda g: g.dma_start(
            out=rbs[:, :], in_=dRB), inc="DW", amt=16)
        wpf_state = {"rd": None}
        wpf_state["tok"] = p.add("gpsimd", lambda g: g.dma_start(
            out=wpf[:, :].rearrange("p (k n) -> p k n", k=KT),
            in_=dWq[:, 0, :, :]),
            inc="DW", amt=16)

        def load_ffn_weights(dep):
            arc_tok[0] = p.add("gpsimd", lambda g: g.dma_start(
                out=arc.rearrange("p (k n) -> p k n", k=KT),
                in_=dARC.rearrange("(k p) n -> p k n", p=128)),
                deps=[dep], inc="DW", amt=16)
            bc_tok[0] = p.add("gpsimd", lambda g: g.dma_start(
                out=bcat, in_=dBC), deps=[dep], inc="DW", amt=16)
            bb_tok[0] = p.add("gpsimd", lambda g: g.dma_start(
                out=bbr, in_=dBB), deps=[dep], inc="DW", amt=16)

        bk_w = [None] * 8
        bk_r = [None] * 8

        def mm_round(bki, parts, deps, tag=""):
            tok = None
            first = True
            for i, (o, l, r, st, sp) in enumerate(parts):
                dd = list(deps) + [bk_r[bki]] if first else []
                last = (i == len(parts) - 1)
                tok = p.add("tensor",
                            lambda t, o=o, l=l, r=r, st=st, sp=sp: t.matmul(
                                o, l, r, start=st, stop=sp),
                            deps=dd, inc="T" if last else None)
                first = False
            bk_w[bki] = tok
            bk_r[bki] = None
            return tok

        def evac(eng, bki, fn, deps=()):
            tok = p.add(eng, fn, deps=list(deps) + [bk_w[bki]],
                        inc={"vector": "V", "scalar": "S"}[eng])
            bk_r[bki] = tok
            return tok

        xt_free = [None, None]
        ao_free = [None, None]
        oTb_free = [None, None]
        wbig_free = [None, None]

        def frees():
            return [p.last.get("T"), p.last.get("V"), p.last.get("S")]

        out_dma_count = [0]

        def phase_p0(s, Hh, first_p0):
            ev_last = None
            fz = frees()
            lt = p.last.get("T")
            p0_rd = [lt, lt]
            mtoks = []
            for tl in range(8):
                tg = Hh * 8 + tl
                pr = tl % 2
                tok_x = p.add("sync", lambda y, s=s, tg=tg, pr=pr: y.dma_start(
                    out=xt[pr][:, :], in_=dx[s, tg * 128:(tg + 1) * 128, :]),
                    deps=[xt_free[pr]] + fz, inc="DX", amt=16)
                sx = col()
                tv1 = p.add("vector", lambda v, pr=pr, sx=sx: v.reduce_sum(
                    sx, xt[pr][:, :], axis=AXX), deps=[tok_x], inc="V")
                sq = col()
                ts1 = p.add("scalar", lambda ss, pr=pr, sq=sq: ss.activation(
                    p0snk[:, :], xt[pr][:, :], AF.Square, accum_out=sq),
                    deps=[tv1], inc="S")
                m = col()
                tv2 = p.add("vector", lambda v, sx=sx, m=m: v.tensor_scalar(
                    m, sx, 1.0 / D, 0.0, ALU.mult, ALU.add), deps=[tv1], inc="V")
                msq = col()
                tv3 = p.add("vector", lambda v, m=m, msq=msq: v.tensor_scalar(
                    msq, m, m, None, ALU.mult), deps=[tv2], inc="V")
                var = col()
                tv4 = p.add("vector", lambda v, sq=sq, msq=msq, var=var:
                            v.scalar_tensor_tensor(var, sq, 1.0 / D, msq,
                                                   ALU.mult, ALU.subtract),
                            deps=[ts1, tv3], inc="V")
                sd = col()
                ts2 = p.add("scalar", lambda ss, var=var, sd=sd: ss.activation(
                    sd, var, AF.Sqrt, bias=np.float32(EPS_LN)), deps=[tv4], inc="S")
                rs = col()
                tv5 = p.add("vector", lambda v, sd=sd, rs=rs: v.reciprocal(
                    rs, sd), deps=[ts2], inc="V")
                tap = p.add("vector", lambda v, pr=pr, m=m, rs=rs:
                            v.tensor_scalar(lnbf[pr][:, :], xt[pr][:, :], m, rs,
                                            ALU.subtract, ALU.mult),
                            deps=[tv5, ts1, p0_rd[pr]], inc="V")
                xt_free[pr] = tap
                for r in range(4):
                    bki = BK_S0 if r % 2 == 0 else BK_S1
                    parts = []
                    for j in range(4):
                        kk = 4 * r + j
                        parts.append((bank[bki][:, j * 128:(j + 1) * 128],
                                      lnbf[pr][:, kk * 128:(kk + 1) * 128],
                                      ident[:, :], True, True))
                    dd = [tap] + (consts_tok if first_p0 else [])
                    mm_round(bki, parts, dd)
                    dst = lnT[:, :].rearrange(
                        "p (k n) -> p k n", k=KT)[:, 4 * r:4 * r + 4,
                                                  tl * 128:tl * 128 + 128]
                    ev_last = evac("scalar", bki, lambda ss, bki=bki, dst=dst:
                                   ss.copy(dst, bank[bki][:, :].rearrange(
                                       "p (k n) -> p k n", k=4)),
                                   deps=[lt])
                p0_rd[pr] = p.last["T"]
                if tl in (3, 7):
                    nt = tl // 4
                    bki = BK_O0 if nt == 0 else BK_O1
                    parts = []
                    for k in range(KT):
                        parts.append((bank[bki][:, :],
                                      wg1[:, k * 128:(k + 1) * 128],
                                      lnT[:, k * TH + nt * 512:
                                          k * TH + nt * 512 + 512],
                                      k == 0, k == KT - 1))
                    mm_round(bki, parts, [ev_last, wtok[0]])
                    mtoks.append(evac(
                        "vector", bki, lambda v, bki=bki, Hh=Hh, nt=nt:
                        v.tensor_copy(midT[:, Hh * TH + nt * 512:
                                           Hh * TH + nt * 512 + 512],
                                      bank[bki][:, :]), deps=fz))
            return [ev_last] + mtoks

        def phase_attn(s, Hh, scan_state):
            fz = frees()
            ct = p.add("sync", lambda y, Hh=Hh: y.dma_start(
                out=cosH[:, :], in_=dCOS[:, Hh * TH:(Hh + 1) * TH]),
                deps=[scan_state.get("rope_last")] + fz, inc="DX", amt=16)
            ct2 = p.add("sync", lambda y, Hh=Hh: y.dma_start(
                out=sinH[:, :], in_=dSIN[:, Hh * TH:(Hh + 1) * TH]),
                deps=[scan_state.get("rope_last")] + fz, inc="DX", amt=16)
            costok = [ct, ct2]
            st0_free = scan_state.setdefault("st0_free", [None, None])
            osp_dma = scan_state.setdefault("osp_dma", [None, None])
            opn_free = scan_state.setdefault("opn_free", [None, None])
            pend_otrs = scan_state.setdefault("pend_otrs", [])
            snaps = scan_state.setdefault("snaps", [])
            for hg in range(NHG):
                rope_tok = {"q": [[None, None], [None, None]],
                            "k": [[None, None], [None, None]]}
                pr_s = 0
                rope_prev = scan_state.get("rope_prev")
                scan_prev = scan_state.get("scan_last")
                for wname, wd, dstT in (("q", dWq, qT), ("k", dWk, kT)):
                    for h in range(HG):
                        gh = hg * HG + h
                        if wname == "q" and h == 0:
                            wv = wpf[:, :]
                            wt = wpf_state["tok"]
                            reg = None
                        else:
                            reg = (2 * (0 if wname == "q" else 1) + h) % 2
                            wv = wbig[:, reg * 2048:(reg + 1) * 2048]
                            wt = p.add("gpsimd", lambda g, wd=wd, gh=gh, wv=wv:
                                       g.dma_start(
                                           out=wv.rearrange("p (k n) -> p k n", k=KT),
                                           in_=wd[:, gh, :, :]),
                                       deps=[wbig_free[reg]], inc="DW", amt=16)
                        wt_cum = ("DW", p.n["DW"])
                        for nt in range(2):
                            bki = BK_O0 if pr_s % 2 == 0 else BK_O1
                            pr_s += 1
                            parts = []
                            for k in range(KT):
                                parts.append((bank[bki][:, :],
                                              wv[:, k * 128:(k + 1) * 128],
                                              lnT[:, k * TH + nt * 512:
                                                  k * TH + nt * 512 + 512],
                                              k == 0, k == KT - 1))
                            mmt = mm_round(bki, parts, [wt, wt_cum, scan_prev]
                                           + scan_state["p0_done"])
                            if reg is None:
                                wpf_state["rd"] = mmt
                            else:
                                wbig_free[reg] = mmt
                            sp = pr_s % 2
                            tsl = evac("scalar", bki, lambda ss, bki=bki, sp=sp:
                                       ss.activation(st0[sp][:, :],
                                                     bank[bki][:, :], AF.Silu),
                                       deps=[st0_free[sp]])
                            dsl = dstT[:, h * TH + nt * 512: h * TH + nt * 512 + 512]
                            cs = cosH[:, nt * 512:nt * 512 + 512]
                            sn = sinH[:, nt * 512:nt * 512 + 512]
                            v1 = p.add("vector", lambda v, sp=sp: v.tensor_scalar(
                                rt2[0:64, :], st0[sp][64:128, :], -1.0, None,
                                ALU.mult), deps=[tsl, rope_prev, scan_prev],
                                inc="V")
                            v2 = p.add("vector", lambda v, sp=sp: v.tensor_copy(
                                rt2[64:128, :], st0[sp][0:64, :]),
                                deps=[tsl], inc="V")
                            v3 = p.add("vector", lambda v, sp=sp, cs=cs:
                                       v.tensor_mul(rt1[:, :], st0[sp][:, :], cs),
                                       deps=[tsl] + costok, inc="V")
                            st0_free[sp] = v3
                            v4 = p.add("vector", lambda v, sn=sn: v.tensor_mul(
                                rt2[:, :], rt2[:, :], sn), deps=[v1, v2],
                                inc="V")
                            v5 = p.add("vector", lambda v, dsl=dsl: v.tensor_add(
                                dsl, rt1[:, :], rt2[:, :]), deps=[v3, v4],
                                inc="V")
                            rope_prev = v5
                            rope_tok[wname][h][nt] = v5
                scan_state["rope_prev"] = rope_prev
                scan_state["rope_last"] = rope_prev
                sgm_tok = [None] * 8

                def gate_tail(c):
                    bki = BK_DS if c % 2 == 0 else BK_KT
                    xp = c % 2
                    xdep = [sgm_tok[c - 2]] if c >= 2 else [p.last.get("V")]
                    th = evac("scalar", bki, lambda ss, bki=bki, xp=xp:
                              ss.activation(xr[xp][:, :], bank[bki][:, 0:256],
                                            AF.Tanh, scale=np.float32(0.5)),
                              deps=xdep)
                    sgm_tok[c] = p.add("vector", lambda v, xp=xp, c=c:
                                       v.tensor_scalar(
                                           sgm[:, c * 256:(c + 1) * 256],
                                           xr[xp][:, :], 0.5, 0.5, ALU.mult,
                                           ALU.add), deps=[th, scan_prev],
                                       inc="V")
                for c in range(8):
                    gc = Hh * 8 + c
                    bki = BK_DS if c % 2 == 0 else BK_KT
                    parts = [(bank[bki][0:128, 0:256],
                              midT[:, gc * 128:(gc + 1) * 128],
                              wg2[:, hg * 256:(hg + 1) * 256], True, True)]
                    mm_round(bki, parts, [wtok[1], scan_prev]
                             + scan_state["p0_done"])
                    if c >= 1:
                        gate_tail(c - 1)
                gate_tail(7)
                wvt = p.add("gpsimd", lambda g, hg=hg: g.dma_start(
                    out=wbig[:, :].rearrange("p (k n) -> p k n", k=KT),
                    in_=dWv[:, hg, :, :]),
                    deps=[wbig_free[0], wbig_free[1]], inc="DW", amt=16)
                nxt_gh = ((hg + 1) % NHG) * HG
                wpf_state["tok"] = p.add(
                    "gpsimd", lambda g, nxt_gh=nxt_gh: g.dma_start(
                        out=wpf[:, :].rearrange("p (k n) -> p k n", k=KT),
                        in_=dWq[:, nxt_gh, :, :]),
                    deps=[wpf_state["rd"], wvt], inc="DW", amt=16)
                if Hh == 0:
                    prev = wpf_state["tok"]
                    for j in (2 * hg, 2 * hg + 1):
                        prev = p.add("gpsimd", lambda g, j=j: g.dma_start(
                            out=WB[:, :].rearrange("p (k n) -> p k n", k=KT)[
                                :, j:j + 1, :],
                            in_=dWo[:, j:j + 1, :]),
                            deps=[scan_state["wb_guard"], prev],
                            inc="DW", amt=16)
                        wb_tok[0] = prev
                seed_tok = None
                if Hh == 1:
                    seed_tok = p.add("vector", lambda v, hg=hg: v.tensor_copy(
                        Sbsd[:, :], Sst[:, hg * 256:(hg + 1) * 256]),
                        deps=[scan_prev, scan_state.get("sb_tok")], inc="V")
                    for h in range(HG):
                        gh = hg * HG + h
                        seed_tok = p.add("scalar", lambda ss, gh=gh, h=h:
                                         ss.mul(Sb[1][:, h * 128:(h + 1) * 128],
                                                Sst[:, gh * 128:(gh + 1) * 128],
                                                float(lamC[gh])),
                                         deps=[scan_prev, seed_tok,
                                               scan_state.get("sb_tok")],
                                         inc="S")
                    scan_state["sb_tok"] = seed_tok
                last_v_mm = None
                for c in range(8):
                    gc = Hh * 8 + c
                    par = gc % 2
                    nt = c // 4
                    sn = snaps[-2] if len(snaps) >= 2 else []
                    parts = []
                    for k in range(KT):
                        parts.append((bank[BK_V][:, 0:256],
                                      lnT[:, k * TH + c * 128: k * TH + c * 128 + 128],
                                      wbig[:, k * 256:(k + 1) * 256],
                                      k == 0, k == KT - 1))
                    mm_round(BK_V, parts, [wvt] + sn)
                    ve = evac("scalar", BK_V, lambda ss, par=par: ss.copy(
                        vb[par][:, :], bank[BK_V][:, 0:256]), deps=sn)
                    parts = []
                    for h in range(HG):
                        qs = qT[:, h * TH + c * 128: h * TH + c * 128 + 128]
                        ks = kT[:, h * TH + c * 128: h * TH + c * 128 + 128]
                        parts.append((bank[BK_S0 + par][:, h * 128:(h + 1) * 128],
                                      ks, qs, h == 0, h == HG - 1))
                    mm_round(BK_S0 + par, parts,
                             [rope_tok["q"][0][nt], rope_tok["q"][1][nt],
                              rope_tok["k"][0][nt], rope_tok["k"][1][nt]])
                    msk = evac("vector", BK_S0 + par, lambda v, par=par, hg=hg:
                               v.tensor_mul(scm[par][:, :],
                                            bank[BK_S0 + par][:, 0:256],
                                            DTm[:, hg * 256:(hg + 1) * 256]),
                               deps=scan_state["tab_tok"] + sn)
                    kdk_tok = None
                    if gc != NCH - 1:
                        parts = []
                        for h in range(HG):
                            ks = kT[:, h * TH + c * 128: h * TH + c * 128 + 128]
                            parts.append((bank[BK_KT][:, h * 128:(h + 1) * 128],
                                          ks, ident[:, :], True, True))
                        mm_round(BK_KT, parts, [rope_tok["k"][0][nt],
                                                rope_tok["k"][1][nt]])
                        for h in range(HG):
                            gh = hg * HG + h
                            kdk_tok = evac("scalar", BK_KT,
                                           lambda ss, par=par, h=h, gh=gh:
                                           ss.mul(
                                               kdk[par][:, h * 128:(h + 1) * 128],
                                               bank[BK_KT][:, h * 128:(h + 1) * 128],
                                               dkc[:, gh:gh + 1]),
                                           deps=[] if h == 0 else [])
                    while len(pend_otrs) >= 2:
                        pend_otrs.pop(0)()
                    qdq_tok = None
                    if gc != 0:
                        q3 = qT[:, :].rearrange("p (h n) -> p h n", h=HG)[
                            :, :, c * 128:c * 128 + 128]
                        d3 = DQm[:, hg * 256:(hg + 1) * 256].rearrange(
                            "p (h n) -> p h n", h=HG)
                        o3 = qdq[par][:, :].rearrange("p (h n) -> p h n", h=HG)
                        qdq_tok = p.add("vector", lambda v, q3=q3, d3=d3, o3=o3:
                                        v.tensor_mul(o3, q3, d3),
                                        deps=[rope_tok["q"][0][nt],
                                              rope_tok["q"][1][nt]]
                                        + scan_state["tab_tok"], inc="V")
                    parts = []
                    for h in range(HG):
                        parts.append((bank[BK_O0 + par][:, h * 128:(h + 1) * 128],
                                      scm[par][:, h * 128:(h + 1) * 128],
                                      vb[par][:, h * 128:(h + 1) * 128],
                                      h == 0, False))
                    if gc != 0:
                        if Hh == 1 and c == 0:
                            for h in range(HG):
                                parts.append((bank[BK_O0 + par][:, h * 128:(h + 1) * 128],
                                              qdq[par][:, h * 128:(h + 1) * 128],
                                              Sbsd[:, h * 128:(h + 1) * 128],
                                              False, False))
                        else:
                            for h in range(HG):
                                parts.append((bank[BK_O0 + par][:, h * 128:(h + 1) * 128],
                                              qdq[par][:, h * 128:(h + 1) * 128],
                                              dSb[(gc + 1) % 2][:, h * 128:(h + 1) * 128],
                                              False, False))
                            if gc >= 2:
                                for h in range(HG):
                                    parts.append((bank[BK_O0 + par][:, h * 128:(h + 1) * 128],
                                                  qdq[par][:, h * 128:(h + 1) * 128],
                                                  Sb[gc % 2][:, h * 128:(h + 1) * 128],
                                                  False, False))
                    o0, l0, r0, s0f, _ = parts[-1]
                    parts[-1] = (o0, l0, r0, s0f, True)
                    odeps = [msk, ve, qdq_tok,
                             scan_state.get("dsb_tok") if gc != 0 else None,
                             scan_state.get("sb2_tok") if gc >= 2 else None,
                             seed_tok if (Hh == 1 and c == 0) else None]
                    mm_round(BK_O0 + par, parts, odeps)
                    if gc != NCH - 1:
                        parts = []
                        for h in range(HG):
                            parts.append((bank[BK_DS][:, h * 128:(h + 1) * 128],
                                          kdk[par][:, h * 128:(h + 1) * 128],
                                          vb[par][:, h * 128:(h + 1) * 128],
                                          h == 0, h == HG - 1))
                        mm_round(BK_DS, parts, [kdk_tok, ve])
                        dsbt = evac("scalar", BK_DS, lambda ss, par=par:
                                    ss.copy(dSb[par][:, :], bank[BK_DS][:, 0:256]),
                                    deps=[scan_state.get("dsb_rd")])
                        scan_state["dsb_tok"] = dsbt
                        up_tok = None
                        for h in range(HG):
                            gh = hg * HG + h
                            up_tok = p.add(
                                "vector", lambda v, gh=gh, h=h:
                                v.scalar_tensor_tensor(
                                    Sst[:, gh * 128:(gh + 1) * 128],
                                    Sst[:, gh * 128:(gh + 1) * 128],
                                    lamC[gh],
                                    bank[BK_DS][:, h * 128:(h + 1) * 128],
                                    ALU.mult, ALU.add),
                                deps=[bk_w[BK_DS], dsbt,
                                      scan_state.get("sb2_tok")], inc="V")
                        bk_r[BK_DS] = up_tok
                        sb2 = None
                        for h in range(HG):
                            gh = hg * HG + h
                            sb2 = p.add("scalar", lambda ss, par=par, gh=gh, h=h:
                                        ss.mul(Sb[par][:, h * 128:(h + 1) * 128],
                                               Sst[:, gh * 128:(gh + 1) * 128],
                                               float(lamC[gh])),
                                        deps=[up_tok, bk_w[BK_O0 + par]],
                                        inc="S")
                        scan_state["sb2_tok"] = sb2
                    xrt = evac("vector", BK_O0 + par, lambda v, par=par, c=c:
                               v.tensor_mul(xr[par][:, :],
                                            bank[BK_O0 + par][:, 0:256],
                                            sgm[:, c * 256:(c + 1) * 256]),
                               deps=[sgm_tok[c]])
                    scan_state["dsb_rd"] = bk_w[BK_O0 + par]
                    rs_tok = []
                    for h in range(HG):
                        msh = col()
                        t1 = p.add("scalar", lambda ss, par=par, h=h, msh=msh:
                                   ss.activation(dum2[:, h * 128:(h + 1) * 128],
                                                 xr[par][:, h * 128:(h + 1) * 128],
                                                 AF.Square, accum_out=msh),
                                   deps=[xrt], inc="S")
                        sdh = col()
                        t2 = p.add("scalar", lambda ss, msh=msh, sdh=sdh:
                                   ss.activation(sdh, msh, AF.Sqrt,
                                                 scale=np.float32(1.0 / HD), bias=np.float32(EPS_RMS)),
                                   deps=[t1], inc="S")
                        rsh = col()
                        t3 = p.add("vector", lambda v, sdh=sdh, rsh=rsh:
                                   v.reciprocal(rsh, sdh), deps=[t2], inc="V")
                        rs_tok.append((t3, rsh))
                    opn_tok = None
                    for h in range(HG):
                        t3, rsh = rs_tok[h]
                        opn_tok = p.add("vector", lambda v, par=par, h=h, rsh=rsh:
                                        v.tensor_scalar(
                                            opn[par][:, h * 128:(h + 1) * 128],
                                            xr[par][:, h * 128:(h + 1) * 128],
                                            rsh, None, ALU.mult),
                                        deps=[t3, opn_free[par]], inc="V")

                    def make_otr(par=par, hg=hg, Hh=Hh, c=c, opn_tok=opn_tok):
                        def go():
                            parts = []
                            for h in range(HG):
                                parts.append((bank[BK_OT][:, h * 128:(h + 1) * 128],
                                              opn[par][:, h * 128:(h + 1) * 128],
                                              ident[:, :], True, True))
                            mm_round(BK_OT, parts, [opn_tok])
                            opn_free[par] = bk_w[BK_OT]
                            sp_tok = None
                            for h in range(HG):
                                gh = hg * HG + h
                                sp_tok = evac(
                                    "vector", BK_OT,
                                    lambda v, par=par, h=h, gh=gh:
                                    v.tensor_scalar(
                                        osp[par][:, h * 128:(h + 1) * 128],
                                        bank[BK_OT][:, h * 128:(h + 1) * 128],
                                        nwc[:, gh:gh + 1], None, ALU.mult),
                                    deps=[osp_dma[par]] if h == 0 else ())
                            dst = s_o[Hh * 8 + c, :,
                                      hg * 256:(hg + 1) * 256]
                            srcv = osp[par][:, :]
                            osp_dma[par] = p.add(
                                "scalar", lambda ss, dst=dst, srcv=srcv:
                                ss.dma_start(out=dst, in_=srcv),
                                deps=[sp_tok], inc="DS", amt=16)
                            if hg == NHG - 1:
                                scan_state.setdefault("osp_grid", {})[
                                    (Hh, c)] = osp_dma[par]
                        return go
                    pend_otrs.append(make_otr())
                    last_v_mm = bk_w[BK_V]
                    snaps.append([p.last.get("T"), p.last.get("V"),
                                  p.last.get("S")])
                wbig_free[0] = last_v_mm
                wbig_free[1] = last_v_mm
                scan_state["scan_last"] = p.last["T"]
            while pend_otrs:
                pend_otrs.pop(0)()
            scan_state["scan_last"] = p.last["T"]
            return

        def phase_wo(s, osp_grid):
            l2_tok = [None] * NCH
            x2_tok = [None] * NCH
            mgt_tok = [None] * NCH
            strip_tok = [None, None]
            pend_l2 = []
            lnbf_rd = [None, None, None]
            g8_prev = [None]
            for t in range(NCH):
                prt = t % 2
                lb = t % 3
                stok = p.add("sync", lambda y, t=t, prt=prt: y.dma_start(
                    out=oTb[prt][:, :], in_=s_o[t, :, :]),
                    deps=[oTb_free[prt], osp_grid[(t // 8, t % 8)]],
                    inc="DX", amt=16)
                xtok = p.add("gpsimd", lambda g, s=s, t=t, prt=prt: g.dma_start(
                    out=xt[prt][:, :], in_=dx[s, t * 128:(t + 1) * 128, :]),
                    deps=[xt_free[prt]], inc="DW", amt=16)
                pcols = []
                last_mm = None
                for ntt in range(4):
                    bki = BK_S0 if ntt % 2 == 0 else BK_S1
                    parts = []
                    for k in range(KT):
                        parts.append((bank[bki][:, :],
                                      oTb[prt][:, k * 128:(k + 1) * 128],
                                      WB[:, k * D + ntt * 512: k * D + ntt * 512 + 512],
                                      k == 0, k == KT - 1))
                    last_mm = mm_round(bki, parts, [stok, wb_tok[0]])
                    pc = col()
                    pcols.append(pc)
                    evac("vector", bki, lambda v, bki=bki, prt=prt, ntt=ntt, pc=pc:
                         v.tensor_scalar(ao[prt][:, ntt * 512:(ntt + 1) * 512],
                                         bank[bki][:, :], 1.0, 0.0, ALU.mult,
                                         ALU.add, accum_out=pc),
                         deps=[ao_free[prt]])
                oTb_free[prt] = last_mm
                ev4 = p.last["V"]
                if len(pend_l2) >= 2:
                    pend_l2.pop(0)()
                s01 = col()
                a1 = p.add("vector", lambda v, a=pcols[0], b=pcols[1], o=s01:
                           v.tensor_add(o, a, b), deps=[ev4], inc="V")
                s23 = col()
                a2 = p.add("vector", lambda v, a=pcols[2], b=pcols[3], o=s23:
                           v.tensor_add(o, a, b), deps=[ev4], inc="V")
                sxa = col()
                a3 = p.add("vector", lambda v, a=s01, b=s23, o=sxa:
                           v.tensor_add(o, a, b), deps=[a1, a2], inc="V")
                sqa = col()
                sq1 = p.add("scalar", lambda ss, prt=prt, sqa=sqa: ss.activation(
                    snk[:, :], ao[prt][:, :], AF.Square, accum_out=sqa),
                    deps=[ev4], inc="S")
                m = col()
                b1 = p.add("vector", lambda v, sxa=sxa, m=m: v.tensor_scalar(
                    m, sxa, 1.0 / D, 0.0, ALU.mult, ALU.add), deps=[a3], inc="V")
                msq = col()
                b2 = p.add("vector", lambda v, m=m, msq=msq: v.tensor_scalar(
                    msq, m, m, None, ALU.mult), deps=[b1], inc="V")
                var = col()
                b3 = p.add("vector", lambda v, sqa=sqa, msq=msq, var=var:
                           v.scalar_tensor_tensor(var, sqa, 1.0 / D, msq,
                                                  ALU.mult, ALU.subtract),
                           deps=[sq1, b2], inc="V")
                sd = col()
                b4 = p.add("scalar", lambda ss, var=var, sd=sd: ss.activation(
                    sd, var, AF.Sqrt, bias=np.float32(EPS_LN)), deps=[b3], inc="S")
                rs = col()
                b5 = p.add("vector", lambda v, sd=sd, rs=rs: v.reciprocal(rs, sd),
                           deps=[b4], inc="V")
                z1 = p.add("vector", lambda v, prt=prt, m=m, rs=rs:
                           v.tensor_scalar(ao[prt][:, :], ao[prt][:, :], m, rs,
                                           ALU.subtract, ALU.mult),
                           deps=[b5, sq1], inc="V")
                sx2 = col()
                z2 = p.add("vector", lambda v, prt=prt, sx2=sx2:
                           v.scalar_tensor_tensor(ao[prt][:, :], ao[prt][:, :],
                                                  1.0, xt[prt][:, :], ALU.mult,
                                                  ALU.add, accum_out=sx2),
                           deps=[z1, xtok], inc="V")
                xt_free[prt] = z2
                x2_tok[t] = p.add("scalar", lambda ss, t=t, prt=prt: ss.dma_start(
                    out=s_x2[t * 128:(t + 1) * 128, :], in_=ao[prt][:, :]),
                    deps=[z2], inc="DS", amt=16)
                sq2c = col()
                c1 = p.add("scalar", lambda ss, prt=prt, sq2c=sq2c: ss.activation(
                    snk[:, :], ao[prt][:, :], AF.Square, accum_out=sq2c),
                    deps=[z2], inc="S")
                m2 = col()
                c2 = p.add("vector", lambda v, sx2=sx2, m2=m2: v.tensor_scalar(
                    m2, sx2, 1.0 / D, 0.0, ALU.mult, ALU.add), deps=[z2], inc="V")
                m2q = col()
                c3 = p.add("vector", lambda v, m2=m2, m2q=m2q: v.tensor_scalar(
                    m2q, m2, m2, None, ALU.mult), deps=[c2], inc="V")
                v2r = col()
                c4 = p.add("vector", lambda v, sq2c=sq2c, m2q=m2q, v2r=v2r:
                           v.scalar_tensor_tensor(v2r, sq2c, 1.0 / D, m2q,
                                                  ALU.mult, ALU.subtract),
                           deps=[c1, c3], inc="V")
                sd2 = col()
                c5 = p.add("scalar", lambda ss, v2r=v2r, sd2=sd2: ss.activation(
                    sd2, v2r, AF.Sqrt, bias=np.float32(EPS_LN)), deps=[c4], inc="S")
                rs2 = col()
                c6 = p.add("vector", lambda v, sd2=sd2, rs2=rs2: v.reciprocal(
                    rs2, sd2), deps=[c5], inc="V")
                cap = p.add("vector", lambda v, lb=lb, prt=prt, m2=m2, rs2=rs2:
                            v.tensor_scalar(lnbf[lb][:, :], ao[prt][:, :], m2,
                                            rs2, ALU.subtract, ALU.mult),
                            deps=[c6, c1, lnbf_rd[lb]], inc="V")
                ao_free[prt] = cap

                def make_l2tr(t=t, prt=prt, lb=lb, cap=cap):
                    def go():
                        l2b = qT if prt == 0 else kT
                        ev_t = None
                        lr = None
                        for r in range(4):
                            bki = BK_O0 if r % 2 == 0 else BK_O1
                            parts = []
                            for j in range(4):
                                kk = 4 * r + j
                                parts.append((bank[bki][:, j * 128:(j + 1) * 128],
                                              lnbf[lb][:, kk * 128:(kk + 1) * 128],
                                              ident[:, :], True, True))
                            lr = mm_round(bki, parts, [cap])
                            dst = l2b[:, :].rearrange("p (k n) -> p k n", k=KT)[
                                :, 4 * r:4 * r + 4, 0:128]
                            ev_t = evac("scalar", bki, lambda ss, bki=bki, dst=dst:
                                        ss.copy(dst, bank[bki][:, :].rearrange(
                                            "p (k n) -> p k n", k=4)),
                                        deps=[strip_tok[prt]])
                        dt_ = s_l2[t, :, :]
                        l2_tok[t] = p.add("scalar", lambda ss, l2b=l2b, dt_=dt_:
                                          ss.dma_start(out=dt_, in_=l2b[:, :]),
                                          deps=[ev_t], inc="DS", amt=16)
                        lnbf_rd[lb] = lr
                        parts = []
                        for k in range(KT):
                            parts.append((bank[BK_V][:, 0:136],
                                          l2b[:, k * 128:(k + 1) * 128],
                                          arc[:, k * 136:(k + 1) * 136],
                                          k == 0, k == KT - 1))
                        rmm = mm_round(BK_V, parts, [ev_t, arc_tok[0]])
                        strip_tok[prt] = [l2_tok[t], rmm]
                        lgt = p.add("vector", lambda v, t=t: v.tensor_tensor(
                            lgall[:, t * 8:(t + 1) * 8],
                            bank[BK_V][:, 128:136], rbs[:, :],
                            ALU.add), deps=[bk_w[BK_V], rb_tok[0]], inc="V")
                        msh = p.add("scalar", lambda ss, t=t: ss.copy(
                            mid16[:, t * 128:(t + 1) * 128],
                            bank[BK_V][:, 0:128]), deps=[bk_w[BK_V], lgt],
                            inc="S")
                        bk_r[BK_V] = msh
                    return go
                pend_l2.append(make_l2tr())
            wo_done = p.last["T"]
            for ntt in range(4):
                wbase_tok[ntt] = p.add("gpsimd", lambda g, ntt=ntt: g.dma_start(
                    out=WB[:, :].rearrange("p (k n) -> p k n", k=KT)[
                        :, :, ntt * 512:(ntt + 1) * 512],
                    in_=dBWT.rearrange("(k p) n -> p k n", p=128)[
                        :, :, ntt * 512:(ntt + 1) * 512]),
                    deps=[wb_tok[0], wo_done], inc="DW", amt=16)
            while pend_l2:
                pend_l2.pop(0)()
            mde_free = [None, None]
            for t in range(NCH):
                lgs = lgall[:, t * 8:(t + 1) * 8]
                m1 = col()
                g2 = p.add("vector", lambda v, lgs=lgs, m1=m1: v.reduce_max(
                    m1, lgs, axis=AXX), deps=[g8_prev[0]], inc="V")
                g3 = p.add("vector", lambda v, lgs=lgs, m1=m1: v.tensor_scalar(
                    g8a[:, :], lgs, m1, None, ALU.is_ge), deps=[g2], inc="V")
                g4 = p.add("vector", lambda v, lgs=lgs, m1=m1: v.tensor_scalar(
                    lgs, lgs, m1, None, ALU.subtract), deps=[g3], inc="V")
                g5 = p.add("vector", lambda v, lgs=lgs: v.scalar_tensor_tensor(
                    g8b[:, :], g8a[:, :], -1e30, lgs, ALU.mult, ALU.add),
                    deps=[g4], inc="V")
                m2g = col()
                g6 = p.add("vector", lambda v, m2g=m2g: v.reduce_max(
                    m2g, g8b[:, :], axis=AXX), deps=[g5], inc="V")
                e2 = col()
                g7 = p.add("scalar", lambda ss, m2g=m2g, e2=e2: ss.activation(
                    e2, m2g, AF.Exp), deps=[g6], inc="S")
                g8 = p.add("scalar", lambda ss, lgs=lgs: ss.activation(
                    g8a[:, :], lgs, AF.Exp), deps=[g4, g5], inc="S")
                g8_prev[0] = g8
                den = col()
                g9 = p.add("vector", lambda v, e2=e2, den=den: v.tensor_scalar(
                    den, e2, 1.0, None, ALU.add), deps=[g7], inc="V")
                rde = col()
                gA = p.add("vector", lambda v, den=den, rde=rde: v.reciprocal(
                    rde, den), deps=[g9], inc="V")
                gB = p.add("vector", lambda v, lgs=lgs, m2g=m2g: v.tensor_scalar(
                    g8b[:, :], lgs, m2g, None, ALU.is_ge), deps=[g6, g8],
                    inc="V")
                gC = p.add("vector", lambda v: v.tensor_mul(
                    g8a[:, :], g8a[:, :], g8b[:, :]), deps=[g8, gB], inc="V")
                gD = p.add("vector", lambda v, rde=rde: v.tensor_scalar(
                    gat[:, :], g8a[:, :], rde, 2.0, ALU.mult, ALU.mult),
                    deps=[gA, gC], inc="V")
                gbc = gat[:, :].broadcast_to((128, 8, 16))
                mid3 = mid16[:, t * 128:(t + 1) * 128].rearrange(
                    "p (e r) -> p e r", e=8)
                out3 = mde2[t % 2][:, :].rearrange("p (e r) -> p e r", e=8)
                me = p.add("vector", lambda v, gbc=gbc, mid3=mid3, out3=out3:
                           v.tensor_tensor(out3, mid3, gbc, ALU.mult),
                           deps=[gD, mde_free[t % 2]], inc="V")
                bkt = BK_KT if t % 2 == 0 else BK_DS
                parts = [(bank[bkt][:, 0:128], mde2[t % 2][:, :],
                          ident[:, :], True, True)]
                mm_round(bkt, parts, [me])
                mde_free[t % 2] = bk_w[bkt]
                mgt_tok[t] = evac("vector", bkt, lambda v, t=t, bkt=bkt:
                                  v.tensor_copy(mgt[:, t * 128:(t + 1) * 128],
                                                bank[bkt][:, 0:128]))
            return l2_tok, x2_tok, mgt_tok

        def phase_ffn(s, l2_tok, x2_tok, mgt_tok):
            for t in range(NCH):
                prt = t % 2
                stok = p.add("sync", lambda y, t=t, prt=prt: y.dma_start(
                    out=oTb[prt][:, :], in_=s_l2[t, :, :]),
                    deps=[oTb_free[prt], l2_tok[t]], inc="DX", amt=16)
                xtok = p.add("gpsimd", lambda g, t=t, prt=prt: g.dma_start(
                    out=xt[prt][:, :], in_=s_x2[t * 128:(t + 1) * 128, :]),
                    deps=[xt_free[prt], x2_tok[t], wbase_tok[3]],
                    inc="DW", amt=16)
                pcols = []
                last_mm = None
                for ntt in range(4):
                    bki = [BK_S0, BK_S1, BK_O0, BK_O1][ntt]
                    parts = []
                    parts.append((bank[bki][:, :],
                                  oTb[prt][:, 0:128],
                                  WB[:, 0 * D + ntt * 512: 0 * D + ntt * 512 + 512],
                                  True, False))
                    parts.append((bank[bki][:, :], onesr[0:1, :],
                                  bbr[0:1, ntt * 512:(ntt + 1) * 512],
                                  False, False))
                    for k in range(1, KT):
                        parts.append((bank[bki][:, :],
                                      oTb[prt][:, k * 128:(k + 1) * 128],
                                      WB[:, k * D + ntt * 512: k * D + ntt * 512 + 512],
                                      False, False))
                    parts.append((bank[bki][:, :],
                                  mgt[:, t * 128:(t + 1) * 128],
                                  bcat[:, ntt * 512:(ntt + 1) * 512],
                                  False, True))
                    last_mm = mm_round(bki, parts,
                                       [stok, wbase_tok[3], bb_tok[0],
                                        bc_tok[0], mgt_tok[t]])
                    pc = col()
                    pcols.append(pc)
                    evac("vector", bki, lambda v, bki=bki, prt=prt, ntt=ntt, pc=pc:
                         v.tensor_scalar(ao[prt][:, ntt * 512:(ntt + 1) * 512],
                                         bank[bki][:, :], 1.0, 0.0, ALU.mult,
                                         ALU.add, accum_out=pc),
                         deps=[ao_free[prt]])
                oTb_free[prt] = last_mm
                ev4 = p.last["V"]
                s01 = col()
                a1 = p.add("vector", lambda v, a=pcols[0], b=pcols[1], o=s01:
                           v.tensor_add(o, a, b), deps=[ev4], inc="V")
                s23 = col()
                a2 = p.add("vector", lambda v, a=pcols[2], b=pcols[3], o=s23:
                           v.tensor_add(o, a, b), deps=[ev4], inc="V")
                sxa = col()
                a3 = p.add("vector", lambda v, a=s01, b=s23, o=sxa:
                           v.tensor_add(o, a, b), deps=[a1, a2], inc="V")
                sqa = col()
                sq1 = p.add("scalar", lambda ss, prt=prt, sqa=sqa: ss.activation(
                    snk[:, :], ao[prt][:, :], AF.Square, accum_out=sqa),
                    deps=[ev4], inc="S")
                m = col()
                b1 = p.add("vector", lambda v, sxa=sxa, m=m: v.tensor_scalar(
                    m, sxa, 1.0 / D, 0.0, ALU.mult, ALU.add), deps=[a3], inc="V")
                msq = col()
                b2 = p.add("vector", lambda v, m=m, msq=msq: v.tensor_scalar(
                    msq, m, m, None, ALU.mult), deps=[b1], inc="V")
                var = col()
                b3 = p.add("vector", lambda v, sqa=sqa, msq=msq, var=var:
                           v.scalar_tensor_tensor(var, sqa, 1.0 / D, msq,
                                                  ALU.mult, ALU.subtract),
                           deps=[sq1, b2], inc="V")
                sd = col()
                b4 = p.add("scalar", lambda ss, var=var, sd=sd: ss.activation(
                    sd, var, AF.Sqrt, bias=np.float32(EPS_LN)), deps=[b3], inc="S")
                rs = col()
                b5 = p.add("vector", lambda v, sd=sd, rs=rs: v.reciprocal(rs, sd),
                           deps=[b4], inc="V")
                z1 = p.add("vector", lambda v, prt=prt, m=m, rs=rs:
                           v.tensor_scalar(ao[prt][:, :], ao[prt][:, :], m, rs,
                                           ALU.subtract, ALU.mult),
                           deps=[b5, sq1], inc="V")
                z2 = p.add("vector", lambda v, prt=prt:
                           v.scalar_tensor_tensor(ao[prt][:, :], ao[prt][:, :],
                                                  1.0, xt[prt][:, :], ALU.mult,
                                                  ALU.add),
                           deps=[z1, xtok], inc="V")
                xt_free[prt] = z2
                otok = p.add("scalar", lambda ss, s=s, t=t, prt=prt: ss.dma_start(
                    out=dout[s, t * 128:(t + 1) * 128, :], in_=ao[prt][:, :]),
                    deps=[z2], inc="DS", amt=16)
                out_dma_count[0] += 1
                ao_free[prt] = otok

        wb_tok = [None, None]
        wbase_tok = [None] * 4
        load_gate_weights(None)

        def reset_free(slot):
            fz = frees()
            for i in range(2):
                old = slot[i]
                slot[i] = (old if isinstance(old, list) else [old]) + fz

        for s in range(NSEQ):
            p.barrier()
            for slot in (xt_free, ao_free, oTb_free, wbig_free):
                reset_free(slot)
            tab_tok = load_tables(frees() + list(ao_free[0]) + list(ao_free[1]))
            zt = p.add("vector", lambda v: v.memset(Sst, 0.0),
                       deps=frees() + list(ao_free[0]), inc="V")
            scan_state = {"sb_tok": None, "scan_last": None,
                          "rope_prev": None, "rope_last": None,
                          "tab_tok": tab_tok}
            for Hh in range(2):
                scan_state["p0_done"] = phase_p0(
                    s, Hh, first_p0=(s == 0 and Hh == 0))
                if Hh == 0:
                    scan_state["wb_guard"] = p.last.get("T")
                phase_attn(s, Hh, scan_state)
            for slot in (xt_free, ao_free, oTb_free):
                reset_free(slot)
            load_ffn_weights(frees())
            l2_tok, x2_tok, mgt_tok = phase_wo(s, scan_state["osp_grid"])
            p.barrier()
            phase_ffn(s, l2_tok, x2_tok, mgt_tok)
            p.barrier()

        p.add("scalar", lambda ss: ss.nop(), deps=[("DS", p.n["DS"])])

        block = es.enter_context(nc.Block())

        def emit(eng_name):
            def body(e):
                for fn, ws, inc, amt in p.ops[eng_name]:
                    for sname, cval in ws:
                        e.wait_ge(sems[sname], cval)
                    if fn is None:
                        continue
                    ins = fn(e)
                    if inc is not None and ins is not None:
                        ins.then_inc(sems[inc], amt)
            return body

        block.tensor(emit("tensor"))
        block.vector(emit("vector"))
        block.scalar(emit("scalar"))
        block.gpsimd(emit("gpsimd"))
        block.sync(emit("sync"))

    return nc


def _consts():
    lam = -(2.0 ** (-8.0 * (np.arange(H, dtype=np.float64) + 1.0) / H))
    i = np.arange(C, dtype=np.float64)
    DTm = np.zeros((128, H * C), np.float64)
    DQm = np.zeros((128, H * C), np.float64)
    dkc = np.zeros((128, H), np.float64)
    for h in range(H):
        diff = i[None, :] - i[:, None]
        DTm[:, h * C:(h + 1) * C] = np.where(diff >= 0, np.exp(lam[h] * diff), 0.0)
        DQm[:, h * C:(h + 1) * C] = np.exp(lam[h] * (i[None, :] + 1.0))
        dkc[:, h] = np.exp(lam[h] * (C - 1.0 - i))
    half = HD // 2
    theta = 10000.0 ** (-2.0 * np.arange(half, dtype=np.float64) / HD)
    ang = np.arange(T, dtype=np.float64)[None, :] * \
        np.concatenate([theta, theta])[:, None]
    cosF = np.cos(ang)
    sinF = np.sin(ang)
    return {
        "DTm": DTm.astype(bf), "DQm": DQm.astype(bf),
        "dkc": dkc.astype(np.float32),
        "cosF": cosF.astype(bf), "sinF": sinF.astype(bf),
        "ident": np.eye(128, dtype=bf),
        "onesr": np.ones((1, 128), dtype=bf),
    }


_NC_CACHE = {}


def kernel(L0_attn_x, L0_attn_Wq, L0_attn_Wk, L0_attn_Wv, L0_attn_Wo,
           L0_attn_Wg1, L0_attn_Wg2, L0_attn_norm_weight,
           L0_ffn_base_w, L0_ffn_base_b, L0_ffn_router_w, L0_ffn_router_b,
           L0_ffn_lora_A, L0_ffn_lora_B):
    x = np.ascontiguousarray(np.asarray(L0_attn_x, dtype=np.float32))
    cst = _consts()
    shared = {
        "WqS": np.ascontiguousarray(
            np.asarray(L0_attn_Wq, dtype=np.float32)
            .reshape(16, 128, 16, 128).transpose(1, 2, 0, 3)),
        "WkS": np.ascontiguousarray(
            np.asarray(L0_attn_Wk, dtype=np.float32)
            .reshape(16, 128, 16, 128).transpose(1, 2, 0, 3)),
        "WvS": np.ascontiguousarray(
            np.asarray(L0_attn_Wv, dtype=np.float32)
            .reshape(16, 128, 8, 256).transpose(1, 2, 0, 3)),
        "WoS": np.ascontiguousarray(
            np.asarray(L0_attn_Wo, dtype=np.float32)
            .reshape(16, 128, 2048).transpose(1, 0, 2)),
        "Wg1": np.ascontiguousarray(L0_attn_Wg1, dtype=np.float32),
        "Wg2": np.ascontiguousarray(L0_attn_Wg2, dtype=np.float32),
        "baseWT": np.ascontiguousarray(
            np.asarray(L0_ffn_base_w, dtype=np.float32).T),
        "ARcat": np.ascontiguousarray(np.concatenate([
            np.asarray(L0_ffn_lora_A, dtype=np.float32)
            .transpose(1, 0, 2).reshape(D, 128),
            np.asarray(L0_ffn_router_w, dtype=np.float32).T], axis=1)),
        "Bcat": np.ascontiguousarray(
            np.asarray(L0_ffn_lora_B, dtype=np.float32).reshape(128, D)),
        "baseb": np.ascontiguousarray(
            np.asarray(L0_ffn_base_b, dtype=np.float32).reshape(1, D)),
        "rbb": np.ascontiguousarray(np.tile(
            np.asarray(L0_ffn_router_b, dtype=np.float32).reshape(1, 8),
            (128, 1))),
        "nwT": np.ascontiguousarray(
            np.asarray(L0_attn_norm_weight, dtype=np.float32)
            .reshape(16, 128).T),
        "ident": cst["ident"], "cosF": cst["cosF"], "sinF": cst["sinF"],
        "DTm": cst["DTm"], "DQm": cst["DQm"], "dkc": cst["dkc"],
        "onesr": cst["onesr"],
    }
    nc = build_program()
    in_maps = []
    for c in range(8):
        m = dict(shared)
        m["x"] = np.ascontiguousarray(x[c * NSEQ:(c + 1) * NSEQ])
        in_maps.append(m)
    res = run_bass_kernel_spmd(nc, in_maps, core_ids=list(range(8)))
    return np.concatenate([r["out"] for r in res.results], axis=0)


if __name__ == "__main__":
    import time
    t0 = time.time()
    nc = build_program()
    print("built", nc.n_instructions(), "instructions in", time.time() - t0, "s")
